# version 24
# baseline (speedup 1.0000x reference)
import numpy as np
import ml_dtypes
import concourse.bass as bass
import concourse.mybir as mybir
from concourse.bass_utils import run_bass_kernel_spmd
from contextlib import ExitStack

F32 = mybir.dt.float32
BF16 = mybir.dt.bfloat16
AX = mybir.AxisListType.X
ALU = mybir.AluOpType
AF = mybir.ActivationFunctionType

B_, S, D = 2, 8192, 1024
TOWN = 2048
EPS = 1e-5
TQ = 512
N_EP = 4
N_HP = 8
N_KC = 64
SG = 2048
N_SG = 4
OWN_SG = 3
TOT_ITER = 2048
N_QB = 24


def ladder(v, src, rs, sc, QA, QB, QC, QD, out):
    s3 = lambda t: t.rearrange("p (b c) -> p b c", c=32)
    bc = lambda t: t.unsqueeze(2).broadcast_to([128, 32, 32])
    v.tensor_tensor(out=s3(QA), in0=s3(src), in1=bc(rs), op=ALU.mult)
    v.tensor_scalar_mul(out=QB, in0=QA, scalar1=-1.0)
    v.tensor_max(out=QB, in0=QA, in1=QB)
    v.tensor_single_scalar(out=QC, in_=QB, scalar=2.0, op=ALU.is_ge)
    v.tensor_single_scalar(out=QD, in_=QB, scalar=4.0, op=ALU.is_ge)
    v.tensor_scalar(out=QB, in0=QC, scalar1=-4.0, scalar2=8.0,
                    op0=ALU.mult, op1=ALU.add)
    v.scalar_tensor_tensor(out=QB, in0=QD, scalar=-2.0, in1=QB,
                           op0=ALU.mult, op1=ALU.add)
    v.tensor_scalar(out=QC, in0=QC, scalar1=0.125, scalar2=0.125,
                    op0=ALU.mult, op1=ALU.add)
    v.scalar_tensor_tensor(out=QC, in0=QD, scalar=0.25, in1=QC,
                           op0=ALU.mult, op1=ALU.add)
    v.tensor_tensor(out=QA, in0=QA, in1=QB, op=ALU.mult)
    v.tensor_scalar(out=QA, in0=QA, scalar1=1.0, scalar2=16.5,
                    op0=ALU.mult, op1=ALU.add)
    first = True
    for b in (32.0, 16.0, 8.0, 4.0, 2.0, 1.0):
        v.tensor_single_scalar(out=QB, in_=(QA if first else QD),
                               scalar=b, op=ALU.is_ge)
        v.scalar_tensor_tensor(out=QD, in0=QB, scalar=-b,
                               in1=(QA if first else QD),
                               op0=ALU.mult, op1=ALU.add)
        first = False
    v.tensor_tensor(out=QA, in0=QA, in1=QD, op=ALU.subtract)
    v.tensor_scalar(out=QA, in0=QA, scalar1=1.0, scalar2=-16.0,
                    op0=ALU.mult, op1=ALU.add)
    v.tensor_tensor(out=QA, in0=QA, in1=QC, op=ALU.mult)
    return v.tensor_tensor(out=s3(out), in0=s3(QA), in1=bc(sc), op=ALU.mult)


def build(debug=False, stop_after=None):
    nc = bass.Bass("TRN2", target_bir_lowering=False)
    dt = nc.dram_tensor

    x_full = dt("x_full", [S, D], F32, kind="ExternalInput").ap()
    x_own = dt("x_own", [TOWN, D], F32, kind="ExternalInput").ap()
    lnw_pc_d = dt("lnw_pc", [128, 8], F32, kind="ExternalInput").ap()
    lnb_pc_d = dt("lnb_pc", [128, 8], F32, kind="ExternalInput").ap()
    lnw_row_d = dt("lnw_row", [1, D], F32, kind="ExternalInput").ap()
    lnb_row_d = dt("lnb_row", [1, D], F32, kind="ExternalInput").ap()
    WinT_d = dt("WinT", [D, 3 * D], F32, kind="ExternalInput").ap()
    bin_d = dt("bin_row", [1, 3 * D], F32, kind="ExternalInput").ap()
    WoT_d = dt("WoT", [D, D], F32, kind="ExternalInput").ap()
    bo_d = dt("bo_row", [1, D], F32, kind="ExternalInput").ap()
    w0T_d = dt("w0T_p", [128, 8, 256], F32, kind="ExternalInput").ap()
    w1_d = dt("w1_p", [128, 8, 256], F32, kind="ExternalInput").ap()
    w1T_d = dt("w1T_p", [128, 8, 256], F32, kind="ExternalInput").ap()
    w2T_d = dt("w2T_p", [128, 8, 256], F32, kind="ExternalInput").ap()
    ffnw_d = dt("ffn_w", [D, D], F32, kind="ExternalInput").ap()
    ffnb_pc_d = dt("ffn_b_pc", [128, 8], F32, kind="ExternalInput").ap()
    identbf_d = dt("ident_bf", [128, 128], BF16, kind="ExternalInput").ap()
    indf_d = dt("indf_in", [128, 8, 16], BF16, kind="ExternalInput").ap()
    indh_d = dt("indh_in", [16, 8, 128], F32, kind="ExternalInput").ap()

    out_d = dt("out", [TOWN, D], F32, kind="ExternalOutput").ap()

    kT_dram = dt("kT_dram", [D, S], BF16, kind="Internal").ap()
    v_dram = dt("v_dram", [S, D], BF16, kind="Internal").ap()
    xn_dram = dt("xn_dram", [TOWN, D], F32, kind="Internal").ap()
    wtf_dram = dt("wtf_dram", [D, 3 * D], BF16, kind="Internal").ap()
    Z_dram = dt("Z_dram", [16, 2048], F32, kind="Internal").ap()

    dbg = {}
    if debug:
        for nm, shp, dty in [
            ("zT", [128, 8, 2048], BF16), ("qT", [128, 8, 2048], BF16),
            ("lactT", [128, 8, 2048], BF16),
            ("xdT", [128, 8, 2048], BF16),
            ("negc8", [128, 16], F32), ("beff", [1, 3 * D], F32),
            ("wdT", [128, 8, 1024], BF16), ("pT", [128, 8, 2048], BF16),
            ("kn2", [16, 4, 512], BF16), ("qn2", [16, 4, 512], BF16),
            ("nrm", [16, 8], F32), ("c_sc", [1, 32], F32),
            ("indf", [128, 8, 16], BF16),
            ("aoT", [128, 16, 1024], BF16), ("xn2tok", [128, 16, 1024], BF16),
        ]:
            dbg[nm] = dt("dbg_" + nm, shp, dty, kind="ExternalOutput").ap()

    with ExitStack() as ctx:
        en = ctx.enter_context

        def sb(name, shape, dtype):
            return en(nc.sbuf_tensor(name, shape, dtype))

        T1 = sb("T1", [128, 8, 2048], BF16)
        T2 = sb("T2", [128, 16, 1024], BF16)
        T3 = sb("T3", [128, 8, 2048], BF16)
        T4 = sb("T4", [128, 8, 2048], BF16)
        WAr = sb("WAr", [128, 8, 1024], BF16)
        SCR = sb("SCR", [128, 14336], BF16)
        QF = sb("QF", [128, 8192], F32)

        xin = [QF[:, 0:1024], QF[:, 1024:2048],
               QF[:, 4096:5120], QF[:, 5120:6144]]
        QA, QB = QF[:, 2048:3072], QF[:, 3072:4096]
        QC, QD = QF[:, 4096:5120], QF[:, 5120:6144]
        QE = QF[:, 6144:7168]
        Wland = QF[:, 2048:5120]

        kstage = [SCR[:, 0:2048], SCR[:, 2048:4096]]
        kstage2 = [SCR[:, 0:1024], SCR[:, 1024:2048]]
        vstage = [SCR[:, 4096:5120], SCR[:, 5120:6144]]
        ksq = SCR[:, 6144:8192]
        zb_t = [SCR[:, 8192:9216], SCR[:, 9216:10240]]
        wraw = SCR[:, 10240:13312]
        junk = SCR[:, 13312:14336]
        lnbb = SCR[:, 13312:13320]
        gsc = [SCR[:, i * 2048:(i + 1) * 2048] for i in range(7)]
        ktb = [SCR[:, 0:512], SCR[:, 512:1024],
               SCR[:, 7184:7696], SCR[:, 7696:8208]]
        vtb = [SCR[:, 1024:1154], SCR[:, 1154:1284]] + \
            [SCR[:, 6404 + i * 130:6404 + (i + 1) * 130] for i in range(6)]
        Pb = [SCR[:, 1284:2308], SCR[:, 2308:3332],
              SCR[:, 4356:5380], SCR[:, 5380:6404]]
        xdst = SCR[:, 3332:4356]
        WoT_sb = SCR[:, 5380:13572].rearrange("p (c o) -> p c o", c=8)
        dstg = [SCR[:, 4356:4868], SCR[:, 4868:5380], SCR[:, 3332:3844]]
        bo_bf = SCR[0:1, 1284:2308]

        stat = sb("stat", [128, 6, 16], F32)
        lnw_s = sb("lnw_s", [128, 8], F32)
        lnb_s = sb("lnb_s", [128, 8], F32)
        beff = QF[0:1, 5120:8192]
        beff_pc = sb("beff_pc", [128, 16], F32)
        bv_bf = sb("bv_bf", [128, 8], BF16)
        boeff_pc = sb("boeff_pc", [128, 8], F32)
        ffnb_s = sb("ffnb_s", [128, 8], F32)
        ffnb_h = sb("ffnb_h", [128, 8], F32)
        WAf = WAr[:].rearrange("p a b -> p (a b)")
        lw0T = WAf[:, 0:2048].rearrange("p (a b) -> p a b", b=256)
        lm1 = WAf[:, 2048:4096].rearrange("p (a b) -> p a b", b=256)
        lw1T = WAf[:, 4096:6144].rearrange("p (a b) -> p a b", b=256)
        lw2T = WAf[:, 6144:8192].rearrange("p (a b) -> p a b", b=256)
        wn0 = gsc[2][:, 0:512].rearrange("p (a b) -> p a b", b=256)
        wn1 = gsc[2][:, 512:1024].rearrange("p (a b) -> p a b", b=256)
        wn2 = gsc[2][:, 1024:1536].rearrange("p (a b) -> p a b", b=256)
        kn2 = SCR[0:16, 2048:4096].rearrange("p (a b) -> p a b", b=512)
        qn2 = SCR[32:48, 2048:4096].rearrange("p (a b) -> p a b", b=512)
        nrm = sb("nrm", [16, 8], F32)
        c_sc = sb("c_sc", [1, 32], F32)
        negc8h = stat[:, 0, 0:16]
        negc8p = stat[:, 1, 0:8]
        ones_f = QF[0:1, 2048:2176]
        identbf = sb("identbf", [128, 128], BF16)
        indf = SCR[:, 10240:10368].rearrange("p (a b) -> p a b", b=16)
        indh = QF[0:16, 2048:3072].rearrange("p (a b) -> p a b", b=128)
        Ztmp = [QF[0:1, 7168:7680], QF[0:1, 7680:8192]]
        rZ = QF[0:16, 0:2048]
        r75 = sb("r75", [128, 1], F32)
        wsc = sb("wsc", [128, 32], F32)
        wrc = sb("wrc", [128, 32], F32)
        xo_t = [QF[:, 2048:3072], QF[:, 3072:4096]]

        P0 = en(nc.psum_tensor("P0", [128, 1024], F32))
        P1 = en(nc.psum_tensor("P1", [128, 1024], F32))
        P2 = en(nc.psum_tensor("P2", [128, 512], F32))
        P3 = en(nc.psum_tensor("P3", [128, 512], F32))
        P4 = en(nc.psum_tensor("P4", [128, 1024], BF16))
        P5 = en(nc.psum_tensor("P5", [128, 1024], BF16))
        BK4 = [P0[:, 0:512], P0[:, 512:1024], P1[:, 0:512], P1[:, 512:1024]]

        class SemC:
            def __init__(self, name):
                self.h = en(nc.semaphore(name))
                self.v = 0
        s_x = SemC("s_x")
        s_g = SemC("s_g")
        s_v = SemC("s_v")
        s_a = SemC("s_a")
        s_t = SemC("s_t")
        s_kv = SemC("s_kv")
        s_qk = SemC("s_qk")
        s_ex = SemC("s_ex")
        s_pv = SemC("s_pv")
        s_ev = SemC("s_ev")

        block = en(nc.Block())
        MM = dict(skip_group_check=True)

        def emit_tail(waits):
            @block.gpsimd
            def _(g, waits=waits, g0=s_g.v):
                for sem, val in waits:
                    g.wait_ge(sem.h, val)
                for j in range(16):
                    g.dma_start(out=out_d[j * 128:(j + 1) * 128, :],
                                in_=QF[:, 0:1024]).then_inc(s_g.h, 16)
                g.wait_ge(s_g.h, g0 + 256)
            s_g.v += 256

        @block.gpsimd
        def _(g):
            for dst, src in [(identbf[:], identbf_d),
                             (lnw_s[:], lnw_pc_d), (lnb_s[:], lnb_pc_d),
                             (beff[:], bin_d),
                             (ffnb_s[:], ffnb_pc_d)]:
                g.dma_start(out=dst, in_=src).then_inc(s_g.h, 16)
        s_g.v += 16 * 5
        G_LD = s_g.v

        @block.vector
        def _(v):
            v.wait_ge(s_g.h, G_LD)
            v.memset(c_sc[:, 0:1], 1.0)
            v.memset(r75[:], 7.5).then_inc(s_v.h, 1)
            v.wait_ge(s_v.h, 1)
            v.reciprocal(out=r75[:], in_=r75[:])
            v.tensor_scalar_mul(out=ffnb_h[:], in0=ffnb_s[:],
                                scalar1=0.5).then_inc(s_v.h, 1)
        s_v.v += 2
        V_I1 = s_v.v


        if stop_after == "ld":
            emit_tail([(s_v, s_v.v), (s_g, s_g.v)])
            return nc

        BPS = [P0[0:1, 0:512], P0[0:1, 512:1024], P1[0:1, 0:512],
               P1[0:1, 512:1024], P2[0:1, 0:512], P3[0:1, 0:512]]
        WL = [QF[:, 2048:5120], QF[:, 5120:8192]]
        wst = [SCR[:, 0:3072], SCR[:, 3072:6144]]
        IV, IT_, IG = [], [], []
        for ic in range(8):
            @block.sync
            def _(sy, ic=ic, vv=(IV[ic - 2] if ic >= 2 else 0)):
                if ic >= 2:
                    sy.wait_ge(s_v.h, vv)
                sy.dma_start(out=WL[ic % 2][:, 0:1536],
                             in_=WinT_d[ic * 128:(ic + 1) * 128, 0:1536]
                             ).then_inc(s_x.h, 16)
            s_x.v += 16
            xv_ = s_x.v

            @block.gpsimd
            def _(g, ic=ic, vv=(IV[ic - 2] if ic >= 2 else 0),
                  sv=(IV[ic - 1] if ic >= 1 else 0)):
                if ic >= 2:
                    g.wait_ge(s_v.h, vv)
                g.dma_start(out=WL[ic % 2][:, 1536:3072],
                            in_=WinT_d[ic * 128:(ic + 1) * 128, 1536:3072]
                            ).then_inc(s_g.h, 16)
                if ic >= 1:
                    g.wait_ge(s_v.h, sv)
                    icm = ic - 1
                    g.dma_start(out=wtf_dram[icm * 128:(icm + 1) * 128, 0:2048],
                                in_=wst[icm % 2][:, 0:2048]).then_inc(s_g.h, 16)
                    g.dma_start(out=wtf_dram[icm * 128:(icm + 1) * 128, 2048:3072],
                                in_=wst[icm % 2][:, 2048:3072]).then_inc(s_g.h, 16)
            s_g.v += 16 if ic == 0 else 48
            IG.append(s_g.v)

            @block.vector
            def _(v, ic=ic, xv=xv_, gh=IG[ic],
                  gsp=(IG[ic - 1] if ic >= 1 else 0),
                  tv=(IT_[ic - 1] if ic >= 1 else 0)):
                v.wait_ge(s_x.h, xv)
                v.wait_ge(s_g.h, gh - 32 if ic >= 1 else gh)
                if ic >= 2:
                    v.wait_ge(s_g.h, gsp)
                if ic >= 1:
                    v.wait_ge(s_t.h, tv)
                v.tensor_copy(out=wraw[:], in_=WL[ic % 2])
                v.tensor_copy(out=lnbb[:, ic:ic + 1], in_=lnb_s[:, ic:ic + 1])
                v.tensor_scalar_mul(out=wst[ic % 2][:, 0:2048],
                                    in0=WL[ic % 2][:, 0:2048],
                                    scalar1=lnw_s[:, ic:ic + 1])
                v.tensor_scalar_mul(out=wst[ic % 2][:, 2048:3072],
                                    in0=WL[ic % 2][:, 2048:3072],
                                    scalar1=lnw_s[:, ic:ic + 1]).then_inc(s_v.h, 1)
            s_v.v += 1
            IV.append(s_v.v)

            @block.tensor
            def _(te, ic=ic, vv=s_v.v):
                te.wait_ge(s_v.h, vv)
                for og in range(6):
                    ins = te.matmul(BPS[og], lnbb[:, ic:ic + 1],
                                    wraw[:, og * 512:(og + 1) * 512],
                                    start=(ic == 0), stop=(ic == 7), **MM)
                ins.then_inc(s_t.h, 1)
            s_t.v += 1
            IT_.append(s_t.v)

        @block.gpsimd
        def _(g, sv=IV[7]):
            g.wait_ge(s_v.h, sv)
            g.dma_start(out=wtf_dram[7 * 128:8 * 128, 0:2048],
                        in_=wst[7 % 2][:, 0:2048]).then_inc(s_g.h, 16)
            g.dma_start(out=wtf_dram[7 * 128:8 * 128, 2048:3072],
                        in_=wst[7 % 2][:, 2048:3072]).then_inc(s_g.h, 16)
        s_g.v += 32

        @block.vector
        def _(v, tv=s_t.v):
            v.wait_ge(s_t.h, tv)
            for og in range(6):
                v.tensor_add(out=beff[:, og * 512:(og + 1) * 512],
                             in0=beff[:, og * 512:(og + 1) * 512], in1=BPS[og])
            v.tensor_copy(out=junk[0:1, 0:1],
                          in_=beff[0:1, 0:1]).then_inc(s_v.h, 1)
        s_v.v += 1
        V_BEFF = s_v.v

        if stop_after == "fold":
            emit_tail([(s_v, s_v.v), (s_g, s_g.v), (s_t, s_t.v), (s_x, s_x.v)])
            return nc

        @block.tensor
        def _(te):
            te.wait_ge(s_v.h, V_BEFF)
            for j in range(24):
                dst = P2[:, j:j + 1] if j < 12 else P3[:, j - 12:j - 11]
                ins = nc.tensor.transpose(dst, beff[0:1, j * 128:(j + 1) * 128],
                                          c_sc[0:1, 0:1])
            ins.then_inc(s_t.h, 1)
        s_t.v += 1
        T_BPC = s_t.v

        @block.vector
        def _(v):
            v.wait_ge(s_t.h, T_BPC)
            v.tensor_copy(out=beff_pc[:, 0:12], in_=P2[:, 0:12])
            v.tensor_copy(out=beff_pc[:, 12:16], in_=P3[:, 0:4])
            v.tensor_copy(out=bv_bf[:], in_=P3[:, 4:12]).then_inc(s_v.h, 1)
        s_v.v += 1
        V_INIT_END = s_v.v

        @block.gpsimd
        def _(g, tb=s_t.v):
            g.wait_ge(s_t.h, tb)
            g.dma_start(out=indf[:], in_=indf_d).then_inc(s_g.h, 16)
            g.dma_start(out=T4[:, 0, 0:1024],
                        in_=lnw_row_d.partition_broadcast(128)).then_inc(s_g.h, 16)
            g.dma_start(out=T4[:, 1, 0:1024],
                        in_=lnb_row_d.partition_broadcast(128)).then_inc(s_g.h, 16)
        s_g.v += 48
        G_INDF = s_g.v

        @block.vector
        def _(v, gi=G_INDF):
            v.wait_ge(s_g.h, gi)
            v.memset(kn2[:].rearrange("a b c -> a (b c)"), 0.0)
            v.memset(qn2[:].rearrange("a b c -> a (b c)"), 0.0).then_inc(s_v.h, 1)
        s_v.v += 1
        V_INDF = s_v.v

        if stop_after == "init":
            emit_tail([(s_v, V_INDF), (s_t, T_BPC), (s_g, s_g.v), (s_x, s_x.v)])
            return nc

        av0, aa0, at0, ax0, ag0 = s_v.v, s_a.v, s_t.v, s_x.v, s_g.v

        V_i1, V_Z, V_E = {}, {}, {}
        V_KMX, V_VEV, V_QMX = {}, {}, {}
        c = av0
        for sg in range(N_SG):
            for j in range(8):
                it = sg * 16 + j
                V_i1[it], V_Z[it], V_E[it] = c + 4, c + 6, c + 7
                c += 7
            for i in range(8):
                it = sg * 16 + 8 + i
                V_i1[it], V_Z[it], V_E[it] = c + 4, c + 6, c + 7
                c += 7
                c += 1
                V_KMX[(sg, i)] = c
            for i in range(8, 16):
                c += 1
                V_KMX[(sg, i)] = c
            for ts in range(16):
                c += 1
                V_VEV[(sg, ts)] = c
            if sg == OWN_SG:
                for i in range(16):
                    c += 1
                    V_QMX[i] = c
        A_V_END = c

        A_1, A_2, A_KEV, A_QEV = {}, {}, {}, {}
        c = aa0
        for sg in range(N_SG):
            for j in range(8):
                it = sg * 16 + j
                A_1[it], A_2[it] = c + 1, c + 2
                c += 2
            for i in range(8):
                it = sg * 16 + 8 + i
                A_1[it], A_2[it] = c + 1, c + 2
                c += 2
                c += 1
                A_KEV[(sg, i)] = c
            for i in range(8, 16):
                c += 1
                A_KEV[(sg, i)] = c
            if sg == OWN_SG:
                for i in range(16):
                    c += 1
                    A_QEV[i] = c
        A_A_END = c

        T_TILE, T_KMM, T_KNRM, T_VMM, T_QMM, T_QNRM = {}, {}, {}, {}, {}, {}
        c = at0
        for sg in range(N_SG):
            for j in range(8):
                c += 1
                T_TILE[sg * 16 + j] = c
            for i in range(8):
                c += 1
                T_KMM[(sg, i)] = c
                if i >= 1:
                    c += 1
                    T_KNRM[(sg, i - 1)] = c
                c += 1
                T_TILE[sg * 16 + 8 + i] = c
            for i in range(8, 16):
                c += 1
                T_KMM[(sg, i)] = c
                c += 1
                T_KNRM[(sg, i - 1)] = c
            c += 1
            T_KNRM[(sg, 15)] = c
            for ts in range(16):
                c += 1
                T_VMM[(sg, ts)] = c
            if sg == OWN_SG:
                for i in range(16):
                    c += 1
                    T_QMM[i] = c
                    if i >= 1:
                        c += 1
                        T_QNRM[i - 1] = c
                c += 1
                T_QNRM[15] = c
        A_T_END = c

        X_TILE, X_WP, X_WPC = {}, {}, {}
        c = ax0
        for sg in range(N_SG):
            c += 16
            X_WP[(sg, 0)] = c
            for j in range(16):
                c += 16
                X_TILE[sg * 16 + j] = c
            for p in range(1, 3 if sg == OWN_SG else 2):
                for icc in range(8):
                    c += 16
                    X_WPC[(sg, p, icc)] = c
                X_WP[(sg, p)] = c
        A_X_END = c

        G_XN, G_K, G_V = {}, {}, {}
        c = ag0
        for sg in range(N_SG):
            if sg == OWN_SG:
                for j in range(8):
                    c += 16
                    G_XN[j] = c
                for i in range(8):
                    c += 16
                    G_XN[8 + i] = c
                    c += 16
                    G_K[(sg, i)] = c
                for i in range(8, 16):
                    c += 16
                    G_K[(sg, i)] = c
            else:
                for i in range(16):
                    c += 16
                    G_K[(sg, i)] = c
            for ts in range(16):
                c += 16
                G_V[(sg, ts)] = c
        A_G_END = c

        @block.sync
        def _(sy):
            def wload(sy, col, wait, split=False):
                sy.wait_ge(s_t.h, wait)
                wsrc = wtf_dram.rearrange("(c p) o -> p c o", p=128)
                if not split:
                    sy.dma_start(out=WAr[:],
                                 in_=wsrc[:, :, col:col + 1024]
                                 ).then_inc(s_x.h, 16)
                    return
                for icc in range(8):
                    sy.dma_start(out=WAr[:, icc, :],
                                 in_=wsrc[:, icc, col:col + 1024]
                                 ).then_inc(s_x.h, 16)

            for sg in range(N_SG):
                wload(sy, 1024, T_BPC if sg == 0 else T_VMM[(sg - 1, 15)])
                for j in range(16):
                    it = sg * 16 + j
                    if it == 2:
                        sy.wait_ge(s_v.h, V_INIT_END)
                    if it >= 4:
                        sy.wait_ge(s_v.h, V_Z[it - 4])
                    sy.dma_start(out=xin[it % 4],
                                 in_=x_full[it * 128:(it + 1) * 128, :]
                                 ).then_inc(s_x.h, 16)
                wload(sy, 2048, T_KMM[(sg, 15)], split=True)
                if sg == OWN_SG:
                    wload(sy, 0, T_VMM[(sg, 15)], split=True)

        @block.scalar
        def _(sc):
            def tile_act(sg, j):
                it = sg * 16 + j
                sl = (0, 1, 4, 5)[it % 4]
                sc.wait_ge(s_x.h, X_TILE[it])
                if it >= 4:
                    sc.wait_ge(s_v.h, V_Z[it - 4])
                sc.activation(out=junk[:, 0:1024], in_=xin[it % 4],
                              func=AF.Identity, accum_out=stat[:, sl, 0:1])
                sc.activation(out=junk[:, 0:1024], in_=xin[it % 4],
                              func=AF.Square,
                              accum_out=stat[:, sl, 1:2]).then_inc(s_a.h, 1)
                sc.wait_ge(s_v.h, V_i1[it])
                sc.activation(out=stat[:, sl, 7:8], in_=stat[:, sl, 5:6],
                              func=AF.Ln)
                ins = sc.activation(out=stat[:, sl, 6:7],
                                    in_=stat[:, sl, 7:8],
                                    func=AF.Exp, scale=-0.5)
                if sg == OWN_SG:
                    sc.activation(out=stat[:, sl, 9:10], in_=stat[:, sl, 4:5],
                                  func=AF.Ln)
                    ins = sc.activation(out=stat[:, sl, 8:9],
                                        in_=stat[:, sl, 9:10],
                                        func=AF.Exp, scale=-0.5)
                ins.then_inc(s_a.h, 1)

            def kev_act(sg, i):
                h4, oc = i // 8, i % 8
                if i >= 2:
                    sc.wait_ge(s_g.h, G_K[(sg, i - 2)])
                elif sg > 0:
                    sc.wait_ge(s_g.h, G_K[(sg - 1, 14 + i)])
                else:
                    sc.wait_ge(s_g.h, ag0)
                sc.wait_ge(s_t.h, T_KMM[(sg, i)])
                if i >= 2:
                    sc.wait_ge(s_t.h, T_KNRM[(sg, i - 2)])
                elif sg > 0:
                    sc.wait_ge(s_t.h, T_KNRM[(sg - 1, 14 + i)])
                bks = [P0[:, 0:512], P0[:, 512:1024]] if i % 2 == 0 else \
                    [P1[:, 0:512], P1[:, 512:1024]]
                for t2 in range(2):
                    sc.activation(out=kstage2[i % 2][:, t2 * 512:(t2 + 1) * 512],
                                  in_=bks[t2], func=AF.Identity,
                                  bias=beff_pc[:, 8 + oc:9 + oc])
                sc.activation(out=ksq[:, (i % 2) * 1024:(i % 2 + 1) * 1024],
                              in_=kstage2[i % 2][:],
                              func=AF.Square).then_inc(s_a.h, 1)

            sc.wait_ge(s_v.h, V_INDF)
            for sg in range(N_SG):
                for j in range(8):
                    tile_act(sg, j)
                for i in range(8):
                    tile_act(sg, 8 + i)
                    kev_act(sg, i)
                for i in range(8, 16):
                    kev_act(sg, i)
                if sg == OWN_SG:
                    for i in range(16):
                        h4, oc = i // 8, i % 8
                        sc.wait_ge(s_t.h, T_QMM[i])
                        if i >= 2:
                            sc.wait_ge(s_t.h, T_QNRM[i - 2])
                        else:
                            sc.wait_ge(s_t.h, T_KNRM[(sg, 14 + i)])
                        bks = [P0[:, 0:512], P0[:, 512:1024]] if i % 2 == 0 else \
                            [P1[:, 0:512], P1[:, 512:1024]]
                        for t2 in range(2):
                            sc.activation(
                                out=T3[:, oc, h4 * 1024 + t2 * 512:
                                       h4 * 1024 + (t2 + 1) * 512],
                                in_=bks[t2], func=AF.Identity,
                                bias=beff_pc[:, oc:oc + 1])
                        sc.activation(out=ksq[:, (i % 2) * 1024:(i % 2 + 1) * 1024],
                                      in_=T3[:, oc, h4 * 1024:(h4 + 1) * 1024],
                                      func=AF.Square).then_inc(s_a.h, 1)

        @block.vector
        def _(v):
            def tile_dve(sg, j):
                    it = sg * 16 + j
                    sl = (0, 1, 4, 5)[it % 4]
                    zsl = it % 2
                    b = V_i1[it] - 4
                    v.wait_ge(s_a.h, A_1[it])
                    if it >= 4:
                        v.wait_ge(s_a.h, A_2[it - 4])
                    v.tensor_scalar_mul(out=stat[:, sl, 2:3], in0=stat[:, sl, 0:1],
                                        scalar1=1.0 / 1024).then_inc(s_v.h, 1)
                    v.wait_ge(s_v.h, b + 1)
                    v.tensor_scalar(out=stat[:, sl, 3:4], in0=stat[:, sl, 2:3],
                                    scalar1=stat[:, sl, 2:3], scalar2=-EPS,
                                    op0=ALU.mult, op1=ALU.add).then_inc(s_v.h, 1)
                    v.wait_ge(s_v.h, b + 2)
                    v.scalar_tensor_tensor(out=stat[:, sl, 4:5],
                                           in0=stat[:, sl, 1:2], scalar=1.0 / 1024,
                                           in1=stat[:, sl, 3:4], op0=ALU.mult,
                                           op1=ALU.subtract).then_inc(s_v.h, 1)
                    v.wait_ge(s_v.h, b + 3)
                    v.tensor_scalar(out=stat[:, sl, 5:6], in0=stat[:, sl, 4:5],
                                    scalar1=1.0 + EPS, scalar2=-EPS,
                                    op0=ALU.mult, op1=ALU.add).then_inc(s_v.h, 1)
                    v.wait_ge(s_a.h, A_2[it])
                    ins = v.scalar_tensor_tensor(out=stat[:, sl, 10:11],
                                                 in0=stat[:, sl, 2:3], scalar=-1.0,
                                                 in1=stat[:, sl, 6:7],
                                                 op0=ALU.mult, op1=ALU.mult)
                    if sg == OWN_SG:
                        ins = v.scalar_tensor_tensor(out=stat[:, sl, 11:12],
                                                     in0=stat[:, sl, 2:3],
                                                     scalar=-1.0,
                                                     in1=stat[:, sl, 8:9],
                                                     op0=ALU.mult, op1=ALU.mult)
                    ins.then_inc(s_v.h, 1)
                    v.wait_ge(s_v.h, b + 5)
                    if sg == OWN_SG and j >= 2:
                        v.wait_ge(s_g.h, G_XN[j - 2])
                    v.tensor_scalar(out=zb_t[zsl][:, 0:1024], in0=xin[it % 4],
                                    scalar1=stat[:, sl, 6:7],
                                    scalar2=stat[:, sl, 10:11],
                                    op0=ALU.mult, op1=ALU.add)
                    if sg == OWN_SG:
                        v.tensor_scalar(out=xo_t[zsl], in0=xin[it % 4],
                                        scalar1=stat[:, sl, 8:9],
                                        scalar2=stat[:, sl, 11:12],
                                        op0=ALU.mult, op1=ALU.add)
                        v.tensor_copy(out=T2[:, j, :], in_=zb_t[zsl][:, 0:1024])
                    v.tensor_copy(out=junk[0:1, 0:1],
                                  in_=zb_t[zsl][0:1, 0:1]).then_inc(s_v.h, 1)
                    v.wait_ge(s_t.h, T_TILE[it])
                    if j == 0 and sg >= 1:
                        last = T_QMM[15] if sg - 1 == OWN_SG else \
                            T_VMM[(sg - 1, 15)]
                        v.wait_ge(s_t.h, last)
                    ps = P4 if it % 2 == 0 else P5
                    v.tensor_copy(out=T1[:, :, j * 128:(j + 1) * 128],
                                  in_=ps[:].rearrange("p (c t) -> p c t", c=8)
                                  ).then_inc(s_v.h, 1)

            def kmax_dve(sg, i):
                h4 = i // 8
                v.wait_ge(s_t.h, T_KNRM[(sg, i)])
                v.tensor_tensor(out=kn2[:, 2 * h4, :], in0=kn2[:, 2 * h4, :],
                                in1=P2[0:16, :], op=ALU.max)
                v.tensor_tensor(out=kn2[:, 2 * h4 + 1, :],
                                in0=kn2[:, 2 * h4 + 1, :], in1=P3[0:16, :],
                                op=ALU.max).then_inc(s_v.h, 1)

            for sg in range(N_SG):
                for j in range(8):
                    tile_dve(sg, j)
                for i in range(8):
                    tile_dve(sg, 8 + i)
                    kmax_dve(sg, i)
                for i in range(8, 16):
                    kmax_dve(sg, i)
                for ts in range(16):
                    v.wait_ge(s_t.h, T_VMM[(sg, ts)])
                    if ts >= 2:
                        v.wait_ge(s_g.h, G_V[(sg, ts - 2)])
                    elif sg > 0:
                        v.wait_ge(s_g.h, G_V[(sg - 1, 14 + ts)])
                    b0 = P0 if ts % 2 == 0 else P1
                    v.tensor_copy(out=vstage[ts % 2][:, 0:1024],
                                  in_=b0[:]).then_inc(s_v.h, 1)
                if sg == OWN_SG:
                    for i in range(16):
                        h4 = i // 8
                        v.wait_ge(s_t.h, T_QNRM[i])
                        v.tensor_tensor(out=qn2[:, 2 * h4, :],
                                        in0=qn2[:, 2 * h4, :], in1=P2[0:16, :],
                                        op=ALU.max)
                        v.tensor_tensor(out=qn2[:, 2 * h4 + 1, :],
                                        in0=qn2[:, 2 * h4 + 1, :],
                                        in1=P3[0:16, :],
                                        op=ALU.max).then_inc(s_v.h, 1)
                        if i == 0:
                            v.wait_ge(s_g.h, G_INDF)
                        v.tensor_mul(out=T2[:, i, :], in0=T2[:, i, :],
                                     in1=T4[:, 0, 0:1024])
                        v.tensor_add(out=T2[:, i, :], in0=T2[:, i, :],
                                     in1=T4[:, 1, 0:1024])
                    for icc in range(8):
                        v.tensor_scalar(out=T1[:, icc, :], in0=T1[:, icc, :],
                                        scalar1=lnw_s[:, icc:icc + 1],
                                        scalar2=lnb_s[:, icc:icc + 1],
                                        op0=ALU.mult, op1=ALU.add)

        @block.tensor
        def _(te):
            def tr_pe(sg, j):
                it = sg * 16 + j
                sl = it % 2
                te.wait_ge(s_v.h, V_Z[it])
                if it >= 2:
                    te.wait_ge(s_v.h, V_E[it - 2])
                ps = P4 if it % 2 == 0 else P5
                for cch in range(8):
                    ins = nc.tensor.transpose(ps[:, cch * 128:(cch + 1) * 128],
                                              zb_t[sl][:, cch * 128:(cch + 1) * 128],
                                              identbf[:])
                ins.then_inc(s_t.h, 1)

            def kg_pe(sg, i):
                h4, oc = i // 8, i % 8
                if i == 0:
                    te.wait_ge(s_v.h, V_E[sg * 16 + 7])
                elif i == 8:
                    te.wait_ge(s_v.h, V_E[sg * 16 + 15])
                if i >= 2:
                    te.wait_ge(s_a.h, A_KEV[(sg, i - 2)])
                elif sg > 0:
                    te.wait_ge(s_v.h, V_VEV[(sg - 1, 14 + i)])
                bks = [P0[:, 0:512], P0[:, 512:1024]] if i % 2 == 0 else \
                    [P1[:, 0:512], P1[:, 512:1024]]
                for icc in range(8):
                    for t2 in range(2):
                        ins = te.matmul(
                            bks[t2], WAr[:, icc, oc * 128:(oc + 1) * 128],
                            T1[:, icc, h4 * 1024 + t2 * 512:
                               h4 * 1024 + (t2 + 1) * 512],
                            start=(icc == 0), stop=(icc == 7), **MM)
                ins.then_inc(s_t.h, 1)

            def knorm_pe(sg, m):
                ocm = m % 8
                te.wait_ge(s_a.h, A_KEV[(sg, m)])
                if m >= 1:
                    te.wait_ge(s_v.h, V_KMX[(sg, m - 1)])
                elif sg > 0:
                    te.wait_ge(s_v.h, V_KMX[(sg - 1, 15)])
                else:
                    te.wait_ge(s_v.h, V_INIT_END)
                hb = (m % 2) * 1024
                ins = te.matmul(P2[0:16, :], indf[:, ocm, :],
                                ksq[:, hb:hb + 512],
                                start=True, stop=True, **MM)
                ins = te.matmul(P3[0:16, :], indf[:, ocm, :],
                                ksq[:, hb + 512:hb + 1024],
                                start=True, stop=True, **MM)
                ins.then_inc(s_t.h, 1)

            for sg in range(N_SG):
                for j in range(8):
                    tr_pe(sg, j)
                te.wait_ge(s_x.h, X_WP[(sg, 0)])
                for i in range(8):
                    kg_pe(sg, i)
                    if i >= 1:
                        knorm_pe(sg, i - 1)
                    tr_pe(sg, 8 + i)
                for i in range(8, 16):
                    kg_pe(sg, i)
                    knorm_pe(sg, i - 1)
                knorm_pe(sg, 15)
                te.wait_ge(s_a.h, A_KEV[(sg, 15)])
                for ts in range(16):
                    if ts >= 2:
                        te.wait_ge(s_v.h, V_VEV[(sg, ts - 2)])
                    elif sg > 0:
                        te.wait_ge(s_v.h, V_VEV[(sg - 1, 14 + ts)])
                    b0 = P0 if ts % 2 == 0 else P1
                    for icc in range(8):
                        if ts == 0:
                            te.wait_ge(s_x.h, X_WPC[(sg, 1, icc)])
                        ins = te.matmul(b0[:, 0:512],
                                        T1[:, icc, ts * 128:(ts + 1) * 128],
                                        WAr[:, icc, 0:512],
                                        start=(icc == 0), stop=(icc == 7), **MM)
                        ins = te.matmul(b0[:, 512:1024],
                                        T1[:, icc, ts * 128:(ts + 1) * 128],
                                        WAr[:, icc, 512:1024],
                                        start=(icc == 0), stop=(icc == 7), **MM)
                    ins.then_inc(s_t.h, 1)
                if sg == OWN_SG:
                    te.wait_ge(s_v.h, V_VEV[(sg, 15)])
                    for i in range(16):
                        h4, oc = i // 8, i % 8
                        if i >= 2:
                            te.wait_ge(s_a.h, A_QEV[i - 2])
                        bks = [P0[:, 0:512], P0[:, 512:1024]] if i % 2 == 0 \
                            else [P1[:, 0:512], P1[:, 512:1024]]
                        for icc in range(8):
                            if i == 0:
                                te.wait_ge(s_x.h, X_WPC[(sg, 2, icc)])
                            for t2 in range(2):
                                ins = te.matmul(
                                    bks[t2],
                                    WAr[:, icc, oc * 128:(oc + 1) * 128],
                                    T1[:, icc, h4 * 1024 + t2 * 512:
                                       h4 * 1024 + (t2 + 1) * 512],
                                    start=(icc == 0), stop=(icc == 7), **MM)
                        ins.then_inc(s_t.h, 1)
                        if i >= 1:
                            m = i - 1
                            ocm = m % 8
                            te.wait_ge(s_a.h, A_QEV[m])
                            if m >= 1:
                                te.wait_ge(s_v.h, V_QMX[m - 1])
                            else:
                                te.wait_ge(s_v.h, V_KMX[(sg, 15)])
                            hb = (m % 2) * 1024
                            ins = te.matmul(P2[0:16, :], indf[:, ocm, :],
                                            ksq[:, hb:hb + 512],
                                            start=True, stop=True, **MM)
                            ins = te.matmul(P3[0:16, :], indf[:, ocm, :],
                                            ksq[:, hb + 512:hb + 1024],
                                            start=True, stop=True, **MM)
                            ins.then_inc(s_t.h, 1)
                    m = 15
                    te.wait_ge(s_a.h, A_QEV[m])
                    te.wait_ge(s_v.h, V_QMX[m - 1])
                    hb = (m % 2) * 1024
                    ins = te.matmul(P2[0:16, :], indf[:, m % 8, :],
                                    ksq[:, hb:hb + 512],
                                    start=True, stop=True, **MM)
                    ins = te.matmul(P3[0:16, :], indf[:, m % 8, :],
                                    ksq[:, hb + 512:hb + 1024],
                                    start=True, stop=True, **MM)
                    ins.then_inc(s_t.h, 1)

        @block.gpsimd
        def _(g):
            def xn_spill(sg, j):
                g.wait_ge(s_v.h, V_Z[sg * 16 + j])
                g.dma_start(out=xn_dram[j * 128:(j + 1) * 128, :],
                            in_=xo_t[(sg * 16 + j) % 2]).then_inc(s_g.h, 16)

            def k_spill(sg, i):
                h4, oc = i // 8, i % 8
                g.wait_ge(s_a.h, A_KEV[(sg, i)])
                g.dma_start(out=kT_dram[oc * 128:(oc + 1) * 128,
                                        sg * SG + h4 * 1024:
                                        sg * SG + (h4 + 1) * 1024],
                            in_=kstage2[i % 2][:]).then_inc(s_g.h, 16)

            for sg in range(N_SG):
                if sg == OWN_SG:
                    for j in range(8):
                        xn_spill(sg, j)
                    for i in range(8):
                        xn_spill(sg, 8 + i)
                        k_spill(sg, i)
                    for i in range(8, 16):
                        k_spill(sg, i)
                else:
                    for i in range(16):
                        k_spill(sg, i)
                for ts in range(16):
                    g.wait_ge(s_v.h, V_VEV[(sg, ts)])
                    g.dma_start(out=v_dram[sg * SG + ts * 128:
                                           sg * SG + (ts + 1) * 128, :],
                                in_=vstage[ts % 2][:, 0:1024]).then_inc(s_g.h, 16)
        s_x.v = A_X_END
        s_g.v = A_G_END
        s_v.v = A_V_END
        s_a.v = A_A_END
        s_t.v = A_T_END

        if debug:
            @block.gpsimd
            def _(g):
                g.wait_ge(s_v.h, A_V_END)
                g.dma_start(out=dbg["zT"], in_=T1[:]).then_inc(s_g.h, 16)
                g.dma_start(out=dbg["qT"], in_=T3[:]).then_inc(s_g.h, 16)
            s_g.v += 32
        G_DBG_A = s_g.v

        @block.vector
        def _(v, v0=s_v.v):
            v.reduce_max(out=nrm[:, 0:4], in_=kn2[:], axis=AX).then_inc(s_v.h, 1)
            v.wait_ge(s_v.h, v0 + 1)
            v.reduce_max(out=nrm[:, 0:1], in_=nrm[:, 0:4],
                         axis=AX).then_inc(s_v.h, 1)
            v.reduce_max(out=nrm[:, 4:8], in_=qn2[:], axis=AX).then_inc(s_v.h, 1)
            v.wait_ge(s_v.h, v0 + 3)
            v.reduce_max(out=nrm[:, 1:2], in_=nrm[:, 4:8],
                         axis=AX).then_inc(s_v.h, 1)
            v.wait_ge(s_v.h, v0 + 4)
            v.tensor_scalar(out=nrm[:, 2:3], in0=nrm[:, 0:1],
                            scalar1=nrm[:, 1:2], scalar2=1e-30,
                            op0=ALU.mult, op1=ALU.add).then_inc(s_v.h, 1)
        s_v.v += 5
        V_NRM = s_v.v

        @block.scalar
        def _(sc):
            sc.wait_ge(s_v.h, V_NRM)
            sc.activation(out=nrm[:, 3:4], in_=nrm[:, 2:3], func=AF.Ln)
            sc.activation(out=nrm[:, 4:5], in_=nrm[:, 3:4], func=AF.Exp,
                          scale=0.5).then_inc(s_a.h, 1)
        s_a.v += 1
        A_C1 = s_a.v

        @block.gpsimd
        def _(g):
            g.wait_ge(s_a.h, A_C1)
            g.dma_start(out=c_sc[0:1, 0:16], in_=nrm[0:16, 4:5],
                        allow_slow_non_contiguous=True).then_inc(s_g.h, 16)
        s_g.v += 16
        G_C1 = s_g.v

        @block.vector
        def _(v, v0=s_v.v):
            v.wait_ge(s_g.h, G_C1)
            v.memset(ones_f[:], 1.0).then_inc(s_v.h, 1)
        s_v.v += 1
        V_C2 = s_v.v

        @block.tensor
        def _(te):
            te.wait_ge(s_v.h, V_C2)
            te.matmul(P3[:, 0:16], ones_f[0:1, :], c_sc[0:1, 0:16],
                      start=True, stop=True, **MM).then_inc(s_t.h, 1)
        s_t.v += 1
        T_C2 = s_t.v

        @block.vector
        def _(v, v0=s_v.v):
            v.wait_ge(s_t.h, T_C2)
            v.tensor_scalar(out=negc8h, in0=P3[:, 0:16], scalar1=-0.125,
                            scalar2=60.0, op0=ALU.mult,
                            op1=ALU.add).then_inc(s_v.h, 1)
            v.wait_ge(s_v.h, v0 + 1)
            hb = stat[:, 0, 0:16].rearrange("p (a b) -> p a b", b=2)
            v.tensor_tensor(out=negc8p, in0=hb[:, :, 0:1], in1=hb[:, :, 1:2],
                            op=ALU.min).then_inc(s_v.h, 1)
        s_v.v += 2
        V_C3 = s_v.v

        if debug:
            @block.gpsimd
            def _(g):
                g.wait_ge(s_v.h, V_C3)
                g.dma_start(out=dbg["kn2"], in_=kn2[:]).then_inc(s_g.h, 16)
                g.dma_start(out=dbg["qn2"], in_=qn2[:]).then_inc(s_g.h, 16)
                g.dma_start(out=dbg["nrm"], in_=nrm[:]).then_inc(s_g.h, 16)
                g.dma_start(out=dbg["c_sc"], in_=c_sc[:]).then_inc(s_g.h, 16)
                g.dma_start(out=dbg["indf"], in_=indf[:]).then_inc(s_g.h, 16)
            s_g.v += 16 * 5

        if stop_after == "A":
            emit_tail([(s_v, s_v.v), (s_t, s_t.v), (s_a, s_a.v),
                       (s_g, s_g.v), (s_x, s_x.v)])
            return nc

        @block.gpsimd
        def _(g, tend=A_T_END):
            g.wait_ge(s_t.h, tend)
            g.dma_start(out=lw0T[:], in_=w0T_d).then_inc(s_g.h, 16)
            g.dma_start(out=lm1[:], in_=w1_d).then_inc(s_g.h, 16)
            g.dma_start(out=lw1T[:], in_=w1T_d).then_inc(s_g.h, 16)
            g.dma_start(out=lw2T[:], in_=w2T_d).then_inc(s_g.h, 16)
        s_g.v += 16 * 4
        G_LACTW = s_g.v

        @block.vector
        def _(v, gdbg=G_LACTW):
            v.wait_ge(s_g.h, gdbg)
            v.tensor_scalar_mul(out=lm1[:].rearrange("p a b -> p (a b)"),
                                in0=lm1[:].rearrange("p a b -> p (a b)"),
                                scalar1=-1.0)
            v.tensor_copy(out=junk[0:1, 0:1],
                          in_=T2[0:1, 0, 0:1]).then_inc(s_v.h, 1)
        s_v.v += 1
        V_FOLD = s_v.v

        if debug:
            @block.gpsimd
            def _(g):
                g.wait_ge(s_v.h, V_FOLD)
                g.dma_start(out=dbg["xn2tok"], in_=T2[:]).then_inc(s_g.h, 16)
            s_g.v += 16

        g_gate, g_g2, g_dh, g_sig, g_hid, g_dg, g_u = gsc
        for u in range(8):
            n, h = u // 4, u % 4
            vbase = s_v.v
            abase = s_a.v
            tbase = s_t.v

            @block.tensor
            def _(te, n=n, h=h, vbase=vbase, abase=abase):
                te.wait_ge(s_v.h, vbase)
                for j in range(8):
                    if j >= 2:
                        jm = j - 2
                        te.wait_ge(s_v.h, vbase + jm + (1 if jm < 4 else 2))
                        te.wait_ge(s_a.h, abase + jm + 1)
                    bg = P0[:, 0:256] if j % 2 == 0 else P0[:, 512:768]
                    b2 = P1[:, 0:256] if j % 2 == 0 else P1[:, 512:768]
                    bd = P2[:, 0:256] if j % 2 == 0 else P3[:, 0:256]
                    for cc in range(2):
                        kv = T1[:, 2 * h + cc,
                                n * 1024 + j * 128:n * 1024 + (j + 1) * 128]
                        ins = te.matmul(bg, kv, lw0T[:, 2 * h + cc, :],
                                        start=(cc == 0), stop=(cc == 1), **MM)
                        ins = te.matmul(b2, kv, lw2T[:, 2 * h + cc, :],
                                        start=(cc == 0), stop=(cc == 1), **MM)
                        ins = te.matmul(bd, kv, lm1[:, 2 * h + cc, :],
                                        start=(cc == 0), stop=(cc == 1), **MM)
                    ins.then_inc(s_t.h, 1)
            s_t.v += 8

            @block.scalar
            def _(sc, tbase=tbase):
                for j in range(8):
                    sc.wait_ge(s_t.h, tbase + j + 1)
                    bg = P0[:, 0:256] if j % 2 == 0 else P0[:, 512:768]
                    sc.activation(out=g_sig[:, j * 256:(j + 1) * 256], in_=bg,
                                  func=AF.Sigmoid).then_inc(s_a.h, 1)
            s_a.v += 8
            A_S1 = s_a.v

            @block.vector
            def _(v, tbase=tbase, A_S1=A_S1, abase=abase):
                for hf in range(2):
                    for j in range(4 * hf, 4 * hf + 4):
                        v.wait_ge(s_t.h, tbase + j + 1)
                        v.wait_ge(s_a.h, abase + j + 1)
                        bg = P0[:, 0:256] if j % 2 == 0 else P0[:, 512:768]
                        b2 = P1[:, 0:256] if j % 2 == 0 else P1[:, 512:768]
                        bd = P2[:, 0:256] if j % 2 == 0 else P3[:, 0:256]
                        v.tensor_copy(out=g_gate[:, j * 256:(j + 1) * 256], in_=bg)
                        v.tensor_copy(out=g_g2[:, j * 256:(j + 1) * 256], in_=b2)
                        v.tensor_copy(out=g_dh[:, j * 256:(j + 1) * 256],
                                      in_=bd).then_inc(s_v.h, 1)
                    cs = slice(hf * 1024, hf * 1024 + 1024)
                    gs, gg, g2_, gd, gu, gh, gdg = (g_sig[:, cs], g_gate[:, cs],
                                                    g_g2[:, cs], g_dh[:, cs],
                                                    g_u[:, cs], g_hid[:, cs],
                                                    g_dg[:, cs])
                    v.tensor_mul(out=gu, in0=gg, in1=gs)
                    v.tensor_mul(out=gh, in0=gu, in1=g2_)
                    v.tensor_mul(out=gdg, in0=gd, in1=gu)
                    v.tensor_mul(out=gu, in0=gd, in1=g2_)
                    v.tensor_mul(out=gd, in0=gs, in1=gs)
                    v.tensor_sub(out=gd, in0=gs, in1=gd)
                    v.tensor_mul(out=gd, in0=gg, in1=gd)
                    v.tensor_add(out=gd, in0=gs, in1=gd)
                    v.tensor_mul(out=gu, in0=gu, in1=gd).then_inc(s_v.h, 1)
            s_v.v += 10
            V_EW0 = s_v.v - 5
            V_EW1 = s_v.v

            @block.tensor
            def _(te, n=n, h=h, vv=V_EW1):
                te.wait_ge(s_v.h, vv)
                for dh_ in range(2):
                    bp = P0[:, dh_ * 512 + 256:dh_ * 512 + 512]
                    for j in range(8):
                        kv = T2[:, 8 * n + j, h * 256 + dh_ * 128:
                                h * 256 + (dh_ + 1) * 128]
                        ins = te.matmul(bp, kv, g_u[:, j * 256:(j + 1) * 256],
                                        start=(j == 0), stop=(j == 7), **MM)
                for dh_ in range(2):
                    bp = P1[:, dh_ * 512 + 256:dh_ * 512 + 512]
                    for j in range(8):
                        kv = T2[:, 8 * n + j, h * 256 + dh_ * 128:
                                h * 256 + (dh_ + 1) * 128]
                        ins = te.matmul(bp, kv, g_dg[:, j * 256:(j + 1) * 256],
                                        start=(j == 0), stop=(j == 7), **MM)
                for eh in range(2):
                    bp = P2[:, 256:512] if eh == 0 else P3[:, 256:512]
                    for j in range(8):
                        hid = g_hid[:, j * 256 + eh * 128:j * 256 + (eh + 1) * 128]
                        kv = T2[:, 8 * n + j, h * 256:(h + 1) * 256]
                        ins = te.matmul(bp, hid, kv,
                                        start=(j == 0), stop=(j == 7), **MM)
                ins.then_inc(s_t.h, 1)
            s_t.v += 1
            T_S2 = s_t.v

            @block.vector
            def _(v, h=h, tv=T_S2):
                v.wait_ge(s_t.h, tv)
                for cc in range(2):
                    v.tensor_sub(out=wn0[:, cc, :], in0=lw0T[:, 2 * h + cc, :],
                                 in1=P0[:, cc * 512 + 256:cc * 512 + 512])
                    v.tensor_sub(out=wn2[:, cc, :], in0=lw2T[:, 2 * h + cc, :],
                                 in1=P1[:, cc * 512 + 256:cc * 512 + 512])
                    v.tensor_add(out=wn1[:, cc, :], in0=lw1T[:, 2 * h + cc, :],
                                 in1=(P2 if cc == 0 else P3)[:, 256:512])
                v.tensor_copy(out=junk[0:1, 0:1],
                              in_=wn0[0:1, 0, 0:1]).then_inc(s_v.h, 1)
            s_v.v += 1
            V_WN = s_v.v

            rnd_v = []
            for eh in range(2):
                @block.tensor
                def _(te, n=n, h=h, eh=eh, wv=V_WN, rv=(rnd_v[0] if eh else None)):
                    te.wait_ge(s_v.h, wv if eh == 0 else rv)
                    for th in range(2):
                        bq = P0[:, th * 512:(th + 1) * 512]
                        b2 = P1[:, th * 512:(th + 1) * 512]
                        for cc in range(2):
                            kv = T1[:, 2 * h + cc,
                                    n * 1024 + th * 512:n * 1024 + (th + 1) * 512]
                            ins = te.matmul(bq, wn0[:, cc, eh * 128:(eh + 1) * 128],
                                            kv, start=(cc == 0), stop=(cc == 1),
                                            **MM)
                            ins = te.matmul(b2, wn2[:, cc, eh * 128:(eh + 1) * 128],
                                            kv, start=(cc == 0), stop=(cc == 1),
                                            **MM)
                    ins.then_inc(s_t.h, 1)
                s_t.v += 1
                T_R = s_t.v

                @block.scalar
                def _(sc, eh=eh, tv=T_R):
                    sc.wait_ge(s_t.h, tv)
                    for th in range(2):
                        sc.activation(out=g_sig[:, eh * 1024 + th * 512:
                                              eh * 1024 + (th + 1) * 512],
                                      in_=P0[:, th * 512:(th + 1) * 512],
                                      func=AF.Sigmoid).then_inc(s_a.h, 1)
                s_a.v += 2
                A_R = s_a.v

                @block.vector
                def _(v, eh=eh, tv=T_R, av=A_R):
                    v.wait_ge(s_t.h, tv)
                    sl_ = slice(eh * 1024, (eh + 1) * 1024)
                    v.tensor_copy(out=g_gate[:, sl_], in_=P0[:])
                    v.tensor_copy(out=g_g2[:, sl_], in_=P1[:])
                    v.wait_ge(s_a.h, av)
                    v.tensor_mul(out=g_gate[:, sl_], in0=g_gate[:, sl_],
                                 in1=g_sig[:, sl_])
                    v.tensor_mul(out=g_u[:, sl_], in0=g_gate[:, sl_],
                                 in1=g_g2[:, sl_]).then_inc(s_v.h, 1)
                s_v.v += 1
                rnd_v.append(s_v.v)

            @block.tensor
            def _(te, n=n, h=h, vv=s_v.v, tb=s_t.v, vb=s_v.v):
                te.wait_ge(s_v.h, vv)
                idx = 0
                for dh_ in range(2):
                    for th in range(2):
                        if idx >= 2:
                            te.wait_ge(s_v.h, vb + (idx - 2) + 1)
                        bp = P2 if idx % 2 == 0 else P3
                        for eh in range(2):
                            ins = te.matmul(bp[:],
                                            wn1[:, eh, dh_ * 128:(dh_ + 1) * 128],
                                            g_u[:, eh * 1024 + th * 512:
                                                eh * 1024 + (th + 1) * 512],
                                            start=(eh == 0), stop=(eh == 1), **MM)
                        ins.then_inc(s_t.h, 1)
                        idx += 1
            s_t.v += 4

            @block.vector
            def _(v, n=n, h=h, t0=s_t.v - 4):
                idx = 0
                for dh_ in range(2):
                    for th in range(2):
                        v.wait_ge(s_t.h, t0 + idx + 1)
                        bp = P2 if idx % 2 == 0 else P3
                        v.tensor_copy(
                            out=T4[:, 2 * h + dh_,
                                   n * 1024 + th * 512:n * 1024 + (th + 1) * 512],
                            in_=bp[:]).then_inc(s_v.h, 1)
                        idx += 1
            s_v.v += 4
        V_B_END = s_v.v
        T_B_END = s_t.v

        if debug:
            @block.gpsimd
            def _(g):
                g.wait_ge(s_v.h, V_B_END)
                g.dma_start(out=dbg["lactT"], in_=T4[:]).then_inc(s_g.h, 16)
            s_g.v += 16

        if stop_after == "B":
            emit_tail([(s_v, s_v.v), (s_t, s_t.v), (s_a, s_a.v),
                       (s_g, s_g.v), (s_x, s_x.v)])
            return nc

        cg0, ct0 = s_g.v, s_t.v
        kv_after = lambda g: 16 * (g + 1) + 16 * (g // 4 + 1)
        A_k = lambda g: 64 * (g // 4) + 16 * (g // 4 + 1)

        @block.vector
        def _(v, vb=V_B_END, gb=s_g.v):
            v.wait_ge(s_g.h, gb)
            for sl_ in range(8):
                v.memset(vtb[sl_][:, 64:65], 1.0)
                v.memset(vtb[sl_][:, 129:130], 1.0)
            v.tensor_copy(out=junk[0:1, 0:1],
                          in_=vtb[0][0:1, 0:1]).then_inc(s_v.h, 1)
        s_v.v += 1
        V_ONES = s_v.v
        cv0 = s_v.v

        V_QSC, V_QLD, V_QTE = {}, {}, {}
        c = cv0
        for u in range(32):
            if u < N_QB:
                V_QSC[u] = c + 3
                V_QLD[u] = c + 4
                c += 4
            if u >= 2 and (u - 2) < N_QB:
                c += 1
                V_QTE[u - 2] = c
        C_V_END = c

        T_QTR = {}
        c = ct0
        for i in range(N_QB):
            c += 1
            T_QTR[i] = c
        C_T_END = c

        G_ZQ = {}
        c = cg0
        for u in range(32):
            c += 32
            if u < N_QB:
                c += 16
            G_ZQ[u] = c
        C_G_END = c

        @block.sync
        def _(sy):
            sy.wait_ge(s_v.h, V_ONES)
            for g in range(TOT_ITER):
                kc = g % 64
                hp = g // (N_EP * N_KC)
                if kc % 4 == 0:
                    if g >= 16:
                        sy.wait_ge(s_qk.h, g - 12)
                    sy.dma_start(out=ktb[(g // 4) % 4],
                                 in_=kT_dram[hp * 128:(hp + 1) * 128,
                                             kc * 128:kc * 128 + 512]
                                 ).then_inc(s_kv.h, 16)
                if g >= 8:
                    sy.wait_ge(s_pv.h, g - 7)
                sy.dma_start(out=vtb[g % 8][:, 0:130]
                             .rearrange("p (h c) -> p h c", h=2)[:, :, 0:64],
                             in_=v_dram[kc * 128:(kc + 1) * 128,
                                        hp * 128:(hp + 1) * 128]
                             .rearrange("k (h c) -> k h c", h=2)
                             ).then_inc(s_kv.h, 16)
        s_kv.v = 16 * TOT_ITER + 16 * (TOT_ITER // 4)

        @block.tensor
        def _(te):
            te.wait_ge(s_v.h, V_ONES)
            for g in range(TOT_ITER):
                kc = g % 64
                u = g // 64
                hp = u // N_EP
                ep = u % N_EP
                if kc == 0 and 2 <= u <= N_QB + 1:
                    i = u - 2
                    te.wait_ge(s_v.h, V_QLD[i])
                    if i >= 2:
                        te.wait_ge(s_v.h, V_QTE[i - 2])
                    ps = P4 if i % 2 == 0 else P5
                    for cch in range(8):
                        ins = nc.tensor.transpose(ps[:, cch * 128:(cch + 1) * 128],
                                                  xdst[:, cch * 128:(cch + 1) * 128],
                                                  identbf[:])
                    ins.then_inc(s_t.h, 1)
                te.wait_ge(s_kv.h, A_k(g))
                if g >= 2:
                    te.wait_ge(s_ex.h, g - 1)
                Sg = P0 if g % 2 == 0 else P1
                kt = ktb[(g // 4) % 4]
                ins = te.matmul(Sg[:, 0:512],
                                kt[0:64, (kc % 4) * 128:(kc % 4 + 1) * 128],
                                T3[0:64, hp, ep * 512:(ep + 1) * 512],
                                start=True, stop=True, tile_position=(0, 0), **MM)
                ins = te.matmul(Sg[:, 512:1024],
                                kt[64:128, (kc % 4) * 128:(kc % 4 + 1) * 128],
                                T3[64:128, hp, ep * 512:(ep + 1) * 512],
                                start=True, stop=True, tile_position=(64, 0), **MM)
                ins.then_inc(s_qk.h, 1)
                if g >= 2:
                    gm = g - 2
                    um = gm // 64
                    te.wait_ge(s_ex.h, gm + 1)
                    te.wait_ge(s_kv.h, kv_after(gm))
                    if gm % 64 == 0:
                        te.wait_ge(s_ev.h, um)
                    ins = te.matmul(P2[0:65, :], vtb[gm % 8][:, 0:65],
                                    Pb[gm % 4][:, 0:512],
                                    start=(gm % 64 == 0), stop=(gm % 64 == 63),
                                    **MM)
                    ins = te.matmul(P3[0:65, :], vtb[gm % 8][:, 65:130],
                                    Pb[gm % 4][:, 512:1024],
                                    start=(gm % 64 == 0), stop=(gm % 64 == 63),
                                    **MM)
                    ins.then_inc(s_pv.h, 1)
            for gm in range(TOT_ITER - 2, TOT_ITER):
                um = gm // 64
                te.wait_ge(s_ex.h, gm + 1)
                te.wait_ge(s_kv.h, kv_after(gm))
                if gm % 64 == 0:
                    te.wait_ge(s_ev.h, um)
                ins = te.matmul(P2[0:65, :], vtb[gm % 8][:, 0:65],
                                Pb[gm % 4][:, 0:512],
                                start=(gm % 64 == 0), stop=(gm % 64 == 63), **MM)
                ins = te.matmul(P3[0:65, :], vtb[gm % 8][:, 65:130],
                                Pb[gm % 4][:, 512:1024],
                                start=(gm % 64 == 0), stop=(gm % 64 == 63), **MM)
                ins.then_inc(s_pv.h, 1)
        s_qk.v = TOT_ITER
        s_pv.v = TOT_ITER
        s_t.v = C_T_END

        @block.scalar
        def _(sc):
            sc.wait_ge(s_v.h, V_C3)
            for g in range(TOT_ITER):
                sc.wait_ge(s_qk.h, g + 1)
                if g >= 4:
                    sc.wait_ge(s_pv.h, g - 3)
                Sg = P0 if g % 2 == 0 else P1
                hp = g // (N_EP * N_KC)
                sc.activation(out=Pb[g % 4][:], in_=Sg[:],
                              func=AF.Exp, bias=negc8p[:, hp:hp + 1],
                              scale=0.125).then_inc(s_ex.h, 1)
        s_ex.v = TOT_ITER

        @block.gpsimd
        def _(g):
            for u in range(32):
                g.wait_ge(s_ev.h, u + 1)
                hp = u // N_EP
                ep = u % N_EP
                g.dma_start(out=Z_dram[2 * hp:2 * hp + 1, ep * 512:(ep + 1) * 512],
                            in_=Ztmp[0]).then_inc(s_g.h, 16)
                g.dma_start(out=Z_dram[2 * hp + 1:2 * hp + 2,
                                       ep * 512:(ep + 1) * 512],
                            in_=Ztmp[1]).then_inc(s_g.h, 16)
                if u < N_QB:
                    if u >= 2:
                        g.wait_ge(s_v.h, V_QLD[u - 2])
                    if u < 16:
                        g.dma_start(out=xin[u % 2],
                                    in_=xn_dram[u * 128:(u + 1) * 128, :]
                                    ).then_inc(s_g.h, 16)
                    else:
                        wt = u - 16
                        g.dma_start(out=QE, in_=ffnw_d[wt * 128:(wt + 1) * 128, :]
                                    ).then_inc(s_g.h, 16)
        s_g.v = C_G_END

        @block.vector
        def _(v):
            for u in range(32):
                hp = u // N_EP
                ep = u % N_EP
                v.wait_ge(s_pv.h, 64 * (u + 1))
                if u >= 1:
                    v.wait_ge(s_g.h, G_ZQ[u - 1])
                cs = slice(hp * 2048 + ep * 512, hp * 2048 + (ep + 1) * 512)
                v.tensor_copy(out=T2[0:64, :, :].rearrange("p a b -> p (a b)")
                              [:, cs], in_=P2[0:64, :])
                v.tensor_copy(out=T2[64:128, :, :].rearrange("p a b -> p (a b)")
                              [:, cs], in_=P3[0:64, :])
                v.tensor_copy(out=Ztmp[0], in_=P2[64:65, :])
                v.tensor_copy(out=Ztmp[1], in_=P3[64:65, :]).then_inc(s_ev.h, 1)
                if u < N_QB:
                    v.wait_ge(s_g.h, G_ZQ[u])
                    if u >= 1:
                        v.wait_ge(s_t.h, T_QTR[u - 1])
                    src = xin[u % 2] if u < 16 else QE
                    v.reduce_max(out=wsc[:],
                                 in_=src.rearrange("p (b c) -> p b c", c=32),
                                 axis=AX,
                                 apply_absolute_value=True).then_inc(s_v.h, 1)
                    v.wait_ge(s_v.h, V_QSC[u] - 2)
                    v.tensor_scalar(out=wsc[:], in0=wsc[:], scalar1=r75[:, 0:1],
                                    scalar2=1e-30, op0=ALU.mult,
                                    op1=ALU.max).then_inc(s_v.h, 1)
                    v.wait_ge(s_v.h, V_QSC[u] - 1)
                    v.reciprocal(out=wrc[:], in_=wsc[:]).then_inc(s_v.h, 1)
                    v.wait_ge(s_v.h, V_QSC[u])
                    ladder(v, src, wrc[:], wsc[:], QA, QB, QC, QD,
                           xdst).then_inc(s_v.h, 1)
                if u >= 2 and (u - 2) < N_QB:
                    i = u - 2
                    v.wait_ge(s_t.h, T_QTR[i])
                    ps = P4 if i % 2 == 0 else P5
                    dstT = T1 if i < 16 else WAr
                    col = i * 128 if i < 16 else (i - 16) * 128
                    v.tensor_copy(out=dstT[:, :, col:col + 128],
                                  in_=ps[:].rearrange("p (c t) -> p c t", c=8)
                                  ).then_inc(s_v.h, 1)
        s_v.v = C_V_END
        s_ev.v = 32

        if debug:
            @block.gpsimd
            def _(g):
                g.wait_ge(s_v.h, C_V_END)
                g.wait_ge(s_ev.h, 32)
                g.wait_ge(s_g.h, C_G_END)
                g.dma_start(out=dbg["xdT"], in_=T1[:]).then_inc(s_g.h, 16)
                g.dma_start(out=dbg["wdT"], in_=WAr[:]).then_inc(s_g.h, 16)
                g.dma_start(out=dbg["aoT"], in_=T2[:]).then_inc(s_g.h, 16)
                g.dma_start(out=dbg["negc8"], in_=negc8h).then_inc(s_g.h, 16)
                g.dma_start(out=dbg["beff"], in_=beff[:]).then_inc(s_g.h, 16)
            s_g.v += 16 * 5
        G_DBG_C = s_g.v

        if stop_after == "C":
            emit_tail([(s_v, s_v.v), (s_t, s_t.v), (s_a, s_a.v),
                       (s_g, s_g.v), (s_x, s_x.v), (s_ev, s_ev.v),
                       (s_pv, s_pv.v), (s_ex, s_ex.v), (s_kv, s_kv.v)])
            return nc

        @block.gpsimd
        def _(g, pvv=s_pv.v, cge=G_DBG_C, cve=C_V_END):
            g.wait_ge(s_pv.h, pvv)
            for icc in range(8):
                g.dma_start(out=WoT_sb[:, icc, :],
                            in_=WoT_d[icc * 128:(icc + 1) * 128, :]
                            ).then_inc(s_g.h, 16)
            g.dma_start(out=bo_bf, in_=bo_d).then_inc(s_g.h, 16)
            g.wait_ge(s_g.h, cge)
            g.dma_start(out=rZ, in_=Z_dram).then_inc(s_g.h, 16)
            g.wait_ge(s_v.h, cve)
            g.dma_start(out=indh[:], in_=indh_d).then_inc(s_g.h, 16)
        s_g.v += 16 * 11
        G_WOT = s_g.v

        @block.vector
        def _(v, gw=G_WOT):
            v.wait_ge(s_g.h, gw)
            v.tensor_scalar_add(out=rZ[:], in0=rZ[:], scalar1=1e-38)
            v.reciprocal(out=rZ[:], in_=rZ[:]).then_inc(s_v.h, 1)
        s_v.v += 1
        V_RZ = s_v.v

        d2_t0 = s_t.v
        d2_v0 = s_v.v
        @block.tensor
        def _(te, t0=d2_t0, v0=d2_v0):
            te.wait_ge(s_v.h, V_RZ)
            for p in range(16):
                icc, tp = p // 2, p % 2
                if p >= 2:
                    te.wait_ge(s_v.h, v0 + (p - 2) + 1)
                bp = P0 if p % 2 == 0 else P1
                te.matmul(bp[:, 0:512], indh[:, icc, :],
                          rZ[0:16, tp * 1024:tp * 1024 + 512],
                          start=True, stop=True, **MM)
                te.matmul(bp[:, 512:1024], indh[:, icc, :],
                          rZ[0:16, tp * 1024 + 512:(tp + 1) * 1024],
                          start=True, stop=True, **MM).then_inc(s_t.h, 1)
        s_t.v += 16

        @block.vector
        def _(v, t0=d2_t0):
            for p in range(16):
                icc, tp = p // 2, p % 2
                v.wait_ge(s_t.h, t0 + p + 1)
                bp = P0 if p % 2 == 0 else P1
                cs = slice(icc * 2048 + tp * 1024, icc * 2048 + (tp + 1) * 1024)
                v.tensor_mul(out=T2[:].rearrange("p a b -> p (a b)")[:, cs],
                             in0=T2[:].rearrange("p a b -> p (a b)")[:, cs],
                             in1=bp[:]).then_inc(s_v.h, 1)
        s_v.v += 16
        V_AOS = s_v.v

        @block.tensor
        def _(te, gv=G_WOT, vv=V_AOS):
            te.wait_ge(s_g.h, gv)
            te.wait_ge(s_v.h, vv)
            for icc in range(8):
                ins = te.matmul(P2[0:1, :], bv_bf[:, icc:icc + 1],
                                WoT_sb[:, icc, 0:512],
                                start=(icc == 0), stop=(icc == 7), **MM)
                ins = te.matmul(P3[0:1, :], bv_bf[:, icc:icc + 1],
                                WoT_sb[:, icc, 512:1024],
                                start=(icc == 0), stop=(icc == 7), **MM)
            ins.then_inc(s_t.h, 1)
        s_t.v += 1

        @block.vector
        def _(v, tv=s_t.v):
            v.wait_ge(s_t.h, tv)
            v.memset(c_sc[:, 0:1], 1.0)
            v.tensor_add(out=beff[:, 0:512], in0=bo_bf[:, 0:512], in1=P2[0:1, :])
            v.tensor_add(out=beff[:, 512:1024], in0=bo_bf[:, 512:1024],
                         in1=P3[0:1, :]).then_inc(s_v.h, 1)
        s_v.v += 1

        @block.tensor
        def _(te, vv=s_v.v):
            te.wait_ge(s_v.h, vv)
            for j in range(8):
                ins = nc.tensor.transpose(P2[:, j:j + 1],
                                          beff[0:1, j * 128:(j + 1) * 128],
                                          c_sc[0:1, 0:1])
            ins.then_inc(s_t.h, 1)
        s_t.v += 1

        @block.vector
        def _(v, tv=s_t.v):
            v.wait_ge(s_t.h, tv)
            v.tensor_copy(out=boeff_pc[:], in_=P2[:, 0:8]).then_inc(s_v.h, 1)
        s_v.v += 1
        V_BO = s_v.v

        tsc = [QF[:, 6144:6656], QF[:, 6656:7168]]
        fsc = [QF[:, 7168:7680], QF[:, 7680:8192]]
        gsc4 = QF[:, 4096:6144]
        d_t0 = s_t.v
        d_a0 = s_a.v
        d_v0 = s_v.v
        TQ = lambda grp, t4: d_t0 + 4 * grp + t4 + 1
        AQ = lambda grp, t4: d_a0 + 4 * grp + t4 + 1
        VQ = lambda grp, t4: d_v0 + 4 * grp + t4 + 1

        VQO = lambda oc, t4: d_v0 + 4 * oc + t4 + 1

        @block.tensor
        def _(te, vv=s_v.v):
            te.wait_ge(s_v.h, vv)
            for grp in range(16):
                oc = grp // 2
                ffn = (grp % 2 == 0)
                for t4 in range(4):
                    bp = BK4[t4]
                    if grp >= 1:
                        te.wait_ge(s_a.h, AQ(grp - 1, t4))
                    for icc in range(8):
                        if ffn:
                            ins = te.matmul(bp,
                                            WAr[:, icc, oc * 128:(oc + 1) * 128],
                                            T1[:, icc, t4 * 512:(t4 + 1) * 512],
                                            start=(icc == 0), stop=(icc == 7),
                                            **MM)
                        else:
                            tf = T2[:].rearrange("p a b -> p (a b)")
                            ins = te.matmul(bp,
                                            WoT_sb[:, icc,
                                                   oc * 128:(oc + 1) * 128],
                                            tf[:, icc * 2048 + t4 * 512:
                                               icc * 2048 + (t4 + 1) * 512],
                                            start=(icc == 0), stop=(icc == 7),
                                            **MM)
                    ins.then_inc(s_t.h, 1)
        s_t.v += 64

        @block.scalar
        def _(sc):
            for grp in range(16):
                oc = grp // 2
                ffn = (grp % 2 == 0)
                for t4 in range(4):
                    sc.wait_ge(s_t.h, TQ(grp, t4))
                    if ffn:
                        if oc >= 1:
                            sc.wait_ge(s_v.h, VQO(oc - 1, t4))
                        sc.activation(out=gsc4[:, t4 * 512:(t4 + 1) * 512],
                                      in_=BK4[t4], func=AF.Silu,
                                      bias=ffnb_s[:, oc:oc + 1]
                                      ).then_inc(s_a.h, 1)
                    else:
                        fl = 4 * oc + t4
                        if fl >= 2:
                            sc.wait_ge(s_v.h, d_v0 + fl - 1)
                        sc.activation(out=tsc[t4 % 2], in_=BK4[t4],
                                      func=AF.Identity,
                                      bias=boeff_pc[:, oc:oc + 1]
                                      ).then_inc(s_a.h, 1)
        s_a.v += 64

        @block.vector
        def _(v):
            for oc in range(8):
                for t4 in range(4):
                    v.wait_ge(s_a.h, AQ(2 * oc + 1, t4))
                    cs = slice(t4 * 512, (t4 + 1) * 512)
                    v.tensor_add(out=tsc[t4 % 2], in0=tsc[t4 % 2],
                                 in1=T4[:, oc, cs])
                    v.tensor_mul(out=T3[:, oc, cs],
                                 in0=tsc[t4 % 2],
                                 in1=gsc4[:, cs]).then_inc(s_v.h, 1)
        s_v.v += 32
        V_PT = s_v.v

        if debug:
            @block.gpsimd
            def _(g):
                g.wait_ge(s_v.h, V_PT)
                g.dma_start(out=dbg["pT"], in_=T3[:]).then_inc(s_g.h, 16)
            s_g.v += 16

        d7_t0 = s_t.v
        d8_x0 = s_x.v
        d8_v0 = s_v.v
        d8_g0 = s_g.v

        @block.tensor
        def _(te, t0=d7_t0, v0=d8_v0, vv=V_PT):
            te.wait_ge(s_v.h, vv)
            for j in range(16):
                if j >= 2:
                    te.wait_ge(s_v.h, v0 + (j - 2) + 1)
                ps = P4 if j % 2 == 0 else P5
                for oc in range(8):
                    ins = nc.tensor.transpose(ps[:, oc * 128:(oc + 1) * 128],
                                              T3[:, oc, j * 128:(j + 1) * 128],
                                              identbf[:])
                ins.then_inc(s_t.h, 1)
        s_t.v += 16

        xl_val, xo_val = {}, {}
        _c = 0
        for j in range(16):
            _c += 16
            xl_val[j] = _c
            if j % 2 == 1 and j >= 3:
                _c += 16
                xo_val[j - 2] = _c
        _c += 16
        xo_val[15] = _c
        X_D8_END = _c

        @block.sync
        def _(sy, v0=d8_v0, t0=d2_t0):
            for j in range(16):
                if j == 0:
                    sy.wait_ge(s_t.h, t0 + 16)
                if j >= 2:
                    sy.wait_ge(s_v.h, v0 + (j - 2) + 1)
                sy.dma_start(out=xin[j % 2],
                             in_=x_own[j * 128:(j + 1) * 128, :]
                             ).then_inc(s_x.h, 16)
                if j % 2 == 1 and j >= 3:
                    m = j - 2
                    sy.wait_ge(s_v.h, v0 + m + 1)
                    sy.dma_start(out=out_d[m * 128:(m + 1) * 128, :],
                                 in_=xo_t[m % 2]).then_inc(s_x.h, 16)
            m = 15
            sy.wait_ge(s_v.h, v0 + m + 1)
            sy.dma_start(out=out_d[m * 128:(m + 1) * 128, :],
                         in_=xo_t[m % 2]).then_inc(s_x.h, 16)
        s_x.v += X_D8_END

        @block.vector
        def _(v, x0=d8_x0, g0=d8_g0, t0=d7_t0, xl=dict(xl_val), xo=dict(xo_val)):
            for j in range(16):
                v.wait_ge(s_x.h, x0 + xl[j])
                v.wait_ge(s_t.h, t0 + j + 1)
                if j >= 2:
                    if j % 2 == 0:
                        v.wait_ge(s_g.h, g0 + 16 * (j // 2))
                    else:
                        v.wait_ge(s_x.h, x0 + xo[j - 2])
                ps = P4 if j % 2 == 0 else P5
                v.tensor_add(out=xo_t[j % 2].rearrange("p (c t) -> p c t", c=8),
                             in0=xin[j % 2].rearrange("p (c t) -> p c t", c=8),
                             in1=ps[:].rearrange("p (c t) -> p c t", c=8)
                             ).then_inc(s_v.h, 1)
        s_v.v += 16

        @block.gpsimd
        def _(g, v0=d8_v0, g0=d8_g0, x0=d8_x0, xe=X_D8_END):
            for j in range(0, 16, 2):
                g.wait_ge(s_v.h, v0 + j + 1)
                g.dma_start(out=out_d[j * 128:(j + 1) * 128, :],
                            in_=xo_t[j % 2]).then_inc(s_g.h, 16)
            g.wait_ge(s_g.h, g0 + 16 * 8)
            g.wait_ge(s_x.h, x0 + xe)
        s_g.v += 16 * 8

    return nc



def _indf_const():
    a = np.zeros((128, 8, 16), np.float32)
    for p in range(128):
        for ic in range(8):
            a[p, ic, 2 * ic + (1 if p >= 64 else 0)] = 1.0
    return a.astype(ml_dtypes.bfloat16)


def _indh_const():
    a = np.zeros((16, 8, 128), np.float32)
    for h in range(16):
        for c in range(8):
            for m in range(128):
                if 2 * c + (1 if m >= 64 else 0) == h:
                    a[h, c, m] = 1.0
    return a


def _prep_host(inputs):
    x = np.asarray(inputs["L0_attn_x"], np.float32)
    lnw = np.asarray(inputs["L0_attn_ln_w"], np.float32)
    lnb = np.asarray(inputs["L0_attn_ln_b"], np.float32)
    Win = np.asarray(inputs["L0_attn_in_proj_w"], np.float32)
    bin_ = np.asarray(inputs["L0_attn_in_proj_b"], np.float32)
    Wo = np.asarray(inputs["L0_attn_out_proj_w"], np.float32)
    bo = np.asarray(inputs["L0_attn_out_proj_b"], np.float32)
    w0 = np.asarray(inputs["L0_attn_w0"], np.float32)
    w1 = np.asarray(inputs["L0_attn_w1"], np.float32)
    w2 = np.asarray(inputs["L0_attn_w2"], np.float32)
    fw = np.asarray(inputs["L0_ffn_weight"], np.float32)
    fb = np.asarray(inputs["L0_ffn_bias"], np.float32)

    def pc(vec):
        return np.ascontiguousarray(vec.reshape(8, 128).T)

    def packw(wh):
        return np.ascontiguousarray(
            wh.reshape(4, 2, 128, 256).transpose(2, 0, 1, 3).reshape(128, 8, 256))

    w0T = np.ascontiguousarray(np.swapaxes(w0, 1, 2))
    w2T = np.ascontiguousarray(np.swapaxes(w2, 1, 2))
    w1T = np.ascontiguousarray(np.swapaxes(w1, 1, 2))

    common = dict(
        lnw_pc=pc(lnw), lnb_pc=pc(lnb),
        lnw_row=np.ascontiguousarray(lnw.reshape(1, -1)),
        lnb_row=np.ascontiguousarray(lnb.reshape(1, -1)),
        WinT=np.ascontiguousarray(Win.T),
        bin_row=np.ascontiguousarray(bin_.reshape(1, -1)),
        WoT=np.ascontiguousarray(Wo.T),
        bo_row=np.ascontiguousarray(bo.reshape(1, -1)),
        w0T_p=packw(w0T), w1_p=packw(w1), w1T_p=packw(w1T), w2T_p=packw(w2T),
        ffn_w=np.ascontiguousarray(fw),
        ffn_b_pc=pc(fb),
        ident_bf=np.eye(128, dtype=np.float32).astype(ml_dtypes.bfloat16),
        indf_in=_indf_const(), indh_in=_indh_const(),
    )
    maps = []
    for core in range(8):
        b = core // 4
        s = core % 4
        xb = x[b]
        perm = np.roll(np.arange(4), 3 - s)
        m = dict(common)
        m["x_full"] = np.ascontiguousarray(
            xb.reshape(4, TOWN, D)[perm].reshape(S, D))
        m["x_own"] = np.ascontiguousarray(xb[s * TOWN:(s + 1) * TOWN])
        maps.append(m)
    return maps


_NC_CACHE = {}


def kernel(**inputs):
    if "prog" not in _NC_CACHE:
        _NC_CACHE["prog"] = build(debug=False)
    nc = _NC_CACHE["prog"]
    maps = _prep_host(inputs)
    res = run_bass_kernel_spmd(nc, maps, core_ids=list(range(8)))
    out = np.empty((B_, S, D), np.float32)
    for core in range(8):
        b, s = core // 4, core % 4
        out[b, s * TOWN:(s + 1) * TOWN] = res.results[core]["out"]
    return out


# version 25 (final)
# speedup vs baseline: 1.0074x; 1.0074x over previous
import numpy as np
import ml_dtypes
import concourse.bass as bass
import concourse.mybir as mybir
from concourse.bass_utils import run_bass_kernel_spmd
from contextlib import ExitStack

F32 = mybir.dt.float32
BF16 = mybir.dt.bfloat16
AX = mybir.AxisListType.X
ALU = mybir.AluOpType
AF = mybir.ActivationFunctionType

B_, S, D = 2, 8192, 1024
TOWN = 2048
EPS = 1e-5
TQ = 512
N_EP = 4
N_HP = 8
N_KC = 64
SG = 2048
N_SG = 4
OWN_SG = 3
TOT_ITER = 2048
N_QB = 24


def ladder(v, src, rs, sc, QA, QB, QC, QD, out):
    s3 = lambda t: t.rearrange("p (b c) -> p b c", c=32)
    bc = lambda t: t.unsqueeze(2).broadcast_to([128, 32, 32])
    v.tensor_tensor(out=s3(QA), in0=s3(src), in1=bc(rs), op=ALU.mult)
    v.tensor_scalar_mul(out=QB, in0=QA, scalar1=-1.0)
    v.tensor_max(out=QB, in0=QA, in1=QB)
    v.tensor_single_scalar(out=QC, in_=QB, scalar=2.0, op=ALU.is_ge)
    v.tensor_single_scalar(out=QD, in_=QB, scalar=4.0, op=ALU.is_ge)
    v.tensor_scalar(out=QB, in0=QC, scalar1=-4.0, scalar2=8.0,
                    op0=ALU.mult, op1=ALU.add)
    v.scalar_tensor_tensor(out=QB, in0=QD, scalar=-2.0, in1=QB,
                           op0=ALU.mult, op1=ALU.add)
    v.tensor_scalar(out=QC, in0=QC, scalar1=0.125, scalar2=0.125,
                    op0=ALU.mult, op1=ALU.add)
    v.scalar_tensor_tensor(out=QC, in0=QD, scalar=0.25, in1=QC,
                           op0=ALU.mult, op1=ALU.add)
    v.tensor_tensor(out=QA, in0=QA, in1=QB, op=ALU.mult)
    v.tensor_scalar(out=QA, in0=QA, scalar1=1.0, scalar2=16.5,
                    op0=ALU.mult, op1=ALU.add)
    first = True
    for b in (32.0, 16.0, 8.0, 4.0, 2.0, 1.0):
        v.tensor_single_scalar(out=QB, in_=(QA if first else QD),
                               scalar=b, op=ALU.is_ge)
        v.scalar_tensor_tensor(out=QD, in0=QB, scalar=-b,
                               in1=(QA if first else QD),
                               op0=ALU.mult, op1=ALU.add)
        first = False
    v.tensor_tensor(out=QA, in0=QA, in1=QD, op=ALU.subtract)
    v.tensor_scalar(out=QA, in0=QA, scalar1=1.0, scalar2=-16.0,
                    op0=ALU.mult, op1=ALU.add)
    v.tensor_tensor(out=QA, in0=QA, in1=QC, op=ALU.mult)
    return v.tensor_tensor(out=s3(out), in0=s3(QA), in1=bc(sc), op=ALU.mult)


def build(debug=False, stop_after=None):
    nc = bass.Bass("TRN2", target_bir_lowering=False)
    dt = nc.dram_tensor

    x_full = dt("x_full", [S, D], F32, kind="ExternalInput").ap()
    x_own = dt("x_own", [TOWN, D], F32, kind="ExternalInput").ap()
    lnw_pc_d = dt("lnw_pc", [128, 8], F32, kind="ExternalInput").ap()
    lnb_pc_d = dt("lnb_pc", [128, 8], F32, kind="ExternalInput").ap()
    lnw_row_d = dt("lnw_row", [1, D], F32, kind="ExternalInput").ap()
    lnb_row_d = dt("lnb_row", [1, D], F32, kind="ExternalInput").ap()
    WinT_d = dt("WinT", [D, 3 * D], F32, kind="ExternalInput").ap()
    bin_d = dt("bin_row", [1, 3 * D], F32, kind="ExternalInput").ap()
    WoT_d = dt("WoT", [D, D], F32, kind="ExternalInput").ap()
    bo_d = dt("bo_row", [1, D], F32, kind="ExternalInput").ap()
    w0T_d = dt("w0T_p", [128, 8, 256], F32, kind="ExternalInput").ap()
    w1_d = dt("w1_p", [128, 8, 256], F32, kind="ExternalInput").ap()
    w1T_d = dt("w1T_p", [128, 8, 256], F32, kind="ExternalInput").ap()
    w2T_d = dt("w2T_p", [128, 8, 256], F32, kind="ExternalInput").ap()
    ffnw_d = dt("ffn_w", [D, D], F32, kind="ExternalInput").ap()
    ffnb_pc_d = dt("ffn_b_pc", [128, 8], F32, kind="ExternalInput").ap()
    identbf_d = dt("ident_bf", [128, 128], BF16, kind="ExternalInput").ap()
    indf_d = dt("indf_in", [128, 8, 16], BF16, kind="ExternalInput").ap()
    indh_d = dt("indh_in", [16, 8, 128], F32, kind="ExternalInput").ap()

    out_d = dt("out", [TOWN, D], F32, kind="ExternalOutput").ap()

    kT_dram = dt("kT_dram", [D, S], BF16, kind="Internal").ap()
    v_dram = dt("v_dram", [S, D], BF16, kind="Internal").ap()
    xn_dram = dt("xn_dram", [TOWN, D], F32, kind="Internal").ap()
    wtf_dram = dt("wtf_dram", [D, 3 * D], BF16, kind="Internal").ap()
    Z_dram = dt("Z_dram", [16, 2048], F32, kind="Internal").ap()

    dbg = {}
    if debug:
        for nm, shp, dty in [
            ("zT", [128, 8, 2048], BF16), ("qT", [128, 8, 2048], BF16),
            ("lactT", [128, 8, 2048], BF16),
            ("xdT", [128, 8, 2048], BF16),
            ("negc8", [128, 16], F32), ("beff", [1, 3 * D], F32),
            ("wdT", [128, 8, 1024], BF16), ("pT", [128, 8, 2048], BF16),
            ("kn2", [16, 4, 512], BF16), ("qn2", [16, 4, 512], BF16),
            ("nrm", [16, 8], F32), ("c_sc", [1, 32], F32),
            ("indf", [128, 8, 16], BF16),
            ("aoT", [128, 16, 1024], BF16), ("xn2tok", [128, 16, 1024], BF16),
        ]:
            dbg[nm] = dt("dbg_" + nm, shp, dty, kind="ExternalOutput").ap()

    with ExitStack() as ctx:
        en = ctx.enter_context

        def sb(name, shape, dtype):
            return en(nc.sbuf_tensor(name, shape, dtype))

        T1 = sb("T1", [128, 8, 2048], BF16)
        T2 = sb("T2", [128, 16, 1024], BF16)
        T3 = sb("T3", [128, 8, 2048], BF16)
        T4 = sb("T4", [128, 8, 2048], BF16)
        WAr = sb("WAr", [128, 8, 1024], BF16)
        SCR = sb("SCR", [128, 14336], BF16)
        QF = sb("QF", [128, 8192], F32)

        xin = [QF[:, 0:1024], QF[:, 1024:2048],
               QF[:, 4096:5120], QF[:, 5120:6144]]
        QA, QB = QF[:, 2048:3072], QF[:, 3072:4096]
        QC, QD = QF[:, 4096:5120], QF[:, 5120:6144]
        QE = QF[:, 6144:7168]
        Wland = QF[:, 2048:5120]

        kstage = [SCR[:, 0:2048], SCR[:, 2048:4096]]
        kstage2 = [SCR[:, 0:1024], SCR[:, 1024:2048]]
        vstage = [SCR[:, 4096:5120], SCR[:, 5120:6144]]
        ksq = SCR[:, 6144:8192]
        zb_t = [SCR[:, 8192:9216], SCR[:, 9216:10240]]
        wraw = SCR[:, 10240:13312]
        junk = SCR[:, 13312:14336]
        lnbb = SCR[:, 13312:13320]
        gsc = [SCR[:, i * 2048:(i + 1) * 2048] for i in range(7)]
        ktb = [SCR[:, 0:512], SCR[:, 512:1024],
               SCR[:, 7184:7696], SCR[:, 7696:8208]]
        vtb = [SCR[:, 1024:1154], SCR[:, 1154:1284]] + \
            [SCR[:, 6404 + i * 130:6404 + (i + 1) * 130] for i in range(6)]
        Pb = [SCR[:, 1284:2308], SCR[:, 2308:3332],
              SCR[:, 4356:5380], SCR[:, 5380:6404]]
        xdst = SCR[:, 3332:4356]
        WoT_sb = SCR[:, 5380:13572].rearrange("p (c o) -> p c o", c=8)
        dstg = [SCR[:, 4356:4868], SCR[:, 4868:5380], SCR[:, 3332:3844]]
        bo_bf = SCR[0:1, 1284:2308]

        stat = sb("stat", [128, 6, 16], F32)
        lnw_s = sb("lnw_s", [128, 8], F32)
        lnb_s = sb("lnb_s", [128, 8], F32)
        beff = QF[0:1, 5120:8192]
        beff_pc = sb("beff_pc", [128, 16], F32)
        bv_bf = sb("bv_bf", [128, 8], BF16)
        boeff_pc = sb("boeff_pc", [128, 8], F32)
        ffnb_s = sb("ffnb_s", [128, 8], F32)
        ffnb_h = sb("ffnb_h", [128, 8], F32)
        WAf = WAr[:].rearrange("p a b -> p (a b)")
        lw0T = WAf[:, 0:2048].rearrange("p (a b) -> p a b", b=256)
        lm1 = WAf[:, 2048:4096].rearrange("p (a b) -> p a b", b=256)
        lw1T = WAf[:, 4096:6144].rearrange("p (a b) -> p a b", b=256)
        lw2T = WAf[:, 6144:8192].rearrange("p (a b) -> p a b", b=256)
        wn0 = gsc[2][:, 0:512].rearrange("p (a b) -> p a b", b=256)
        wn1 = gsc[2][:, 512:1024].rearrange("p (a b) -> p a b", b=256)
        wn2 = gsc[2][:, 1024:1536].rearrange("p (a b) -> p a b", b=256)
        kn2 = SCR[0:16, 2048:4096].rearrange("p (a b) -> p a b", b=512)
        qn2 = SCR[32:48, 2048:4096].rearrange("p (a b) -> p a b", b=512)
        nrm = sb("nrm", [16, 8], F32)
        c_sc = sb("c_sc", [1, 32], F32)
        negc8h = stat[:, 0, 0:16]
        negc8p = stat[:, 1, 0:8]
        ones_f = QF[0:1, 2048:2176]
        identbf = sb("identbf", [128, 128], BF16)
        indf = SCR[:, 10240:10368].rearrange("p (a b) -> p a b", b=16)
        indh = QF[0:16, 2048:3072].rearrange("p (a b) -> p a b", b=128)
        Ztmp = [QF[0:1, 7168:7680], QF[0:1, 7680:8192]]
        rZ = QF[0:16, 0:2048]
        r75 = sb("r75", [128, 1], F32)
        wsc = sb("wsc", [128, 32], F32)
        wrc = sb("wrc", [128, 32], F32)
        xo_t = [QF[:, 2048:3072], QF[:, 3072:4096]]

        P0 = en(nc.psum_tensor("P0", [128, 1024], F32))
        P1 = en(nc.psum_tensor("P1", [128, 1024], F32))
        P2 = en(nc.psum_tensor("P2", [128, 512], F32))
        P3 = en(nc.psum_tensor("P3", [128, 512], F32))
        P4 = en(nc.psum_tensor("P4", [128, 1024], BF16))
        P5 = en(nc.psum_tensor("P5", [128, 1024], BF16))
        BK4 = [P0[:, 0:512], P0[:, 512:1024], P1[:, 0:512], P1[:, 512:1024]]

        class SemC:
            def __init__(self, name):
                self.h = en(nc.semaphore(name))
                self.v = 0
        s_x = SemC("s_x")
        s_g = SemC("s_g")
        s_v = SemC("s_v")
        s_a = SemC("s_a")
        s_t = SemC("s_t")
        s_kv = SemC("s_kv")
        s_qk = SemC("s_qk")
        s_ex = SemC("s_ex")
        s_pv = SemC("s_pv")
        s_ev = SemC("s_ev")

        block = en(nc.Block())
        MM = dict(skip_group_check=True)

        def emit_tail(waits):
            @block.gpsimd
            def _(g, waits=waits, g0=s_g.v):
                for sem, val in waits:
                    g.wait_ge(sem.h, val)
                for j in range(16):
                    g.dma_start(out=out_d[j * 128:(j + 1) * 128, :],
                                in_=QF[:, 0:1024]).then_inc(s_g.h, 16)
                g.wait_ge(s_g.h, g0 + 256)
            s_g.v += 256

        @block.gpsimd
        def _(g):
            for dst, src in [(identbf[:], identbf_d),
                             (lnw_s[:], lnw_pc_d), (lnb_s[:], lnb_pc_d),
                             (beff[:], bin_d),
                             (ffnb_s[:], ffnb_pc_d)]:
                g.dma_start(out=dst, in_=src).then_inc(s_g.h, 16)
        s_g.v += 16 * 5
        G_LD = s_g.v

        @block.vector
        def _(v):
            v.wait_ge(s_g.h, G_LD)
            v.memset(c_sc[:, 0:1], 1.0)
            v.memset(r75[:], 7.5).then_inc(s_v.h, 1)
            v.wait_ge(s_v.h, 1)
            v.reciprocal(out=r75[:], in_=r75[:])
            v.tensor_scalar_mul(out=ffnb_h[:], in0=ffnb_s[:],
                                scalar1=0.5).then_inc(s_v.h, 1)
        s_v.v += 2
        V_I1 = s_v.v


        if stop_after == "ld":
            emit_tail([(s_v, s_v.v), (s_g, s_g.v)])
            return nc

        BPS = [P0[0:1, 0:512], P0[0:1, 512:1024], P1[0:1, 0:512],
               P1[0:1, 512:1024], P2[0:1, 0:512], P3[0:1, 0:512]]
        WL = [QF[:, 2048:5120], QF[:, 5120:8192]]
        wst = [SCR[:, 0:3072], SCR[:, 3072:6144]]
        IV, IT_, IG = [], [], []
        for ic in range(8):
            @block.sync
            def _(sy, ic=ic, vv=(IV[ic - 2] if ic >= 2 else 0)):
                if ic >= 2:
                    sy.wait_ge(s_v.h, vv)
                sy.dma_start(out=WL[ic % 2][:, 0:1536],
                             in_=WinT_d[ic * 128:(ic + 1) * 128, 0:1536]
                             ).then_inc(s_x.h, 16)
            s_x.v += 16
            xv_ = s_x.v

            @block.gpsimd
            def _(g, ic=ic, vv=(IV[ic - 2] if ic >= 2 else 0),
                  sv=(IV[ic - 1] if ic >= 1 else 0)):
                if ic >= 2:
                    g.wait_ge(s_v.h, vv)
                g.dma_start(out=WL[ic % 2][:, 1536:3072],
                            in_=WinT_d[ic * 128:(ic + 1) * 128, 1536:3072]
                            ).then_inc(s_g.h, 16)
                if ic >= 1:
                    g.wait_ge(s_v.h, sv)
                    icm = ic - 1
                    g.dma_start(out=wtf_dram[icm * 128:(icm + 1) * 128, 0:2048],
                                in_=wst[icm % 2][:, 0:2048]).then_inc(s_g.h, 16)
                    g.dma_start(out=wtf_dram[icm * 128:(icm + 1) * 128, 2048:3072],
                                in_=wst[icm % 2][:, 2048:3072]).then_inc(s_g.h, 16)
            s_g.v += 16 if ic == 0 else 48
            IG.append(s_g.v)

            @block.vector
            def _(v, ic=ic, xv=xv_, gh=IG[ic],
                  gsp=(IG[ic - 1] if ic >= 1 else 0),
                  tv=(IT_[ic - 1] if ic >= 1 else 0)):
                v.wait_ge(s_x.h, xv)
                v.wait_ge(s_g.h, gh - 32 if ic >= 1 else gh)
                if ic >= 2:
                    v.wait_ge(s_g.h, gsp)
                if ic >= 1:
                    v.wait_ge(s_t.h, tv)
                v.tensor_copy(out=wraw[:], in_=WL[ic % 2])
                v.tensor_copy(out=lnbb[:, ic:ic + 1], in_=lnb_s[:, ic:ic + 1])
                v.tensor_scalar_mul(out=wst[ic % 2][:, 0:2048],
                                    in0=WL[ic % 2][:, 0:2048],
                                    scalar1=lnw_s[:, ic:ic + 1])
                v.tensor_scalar_mul(out=wst[ic % 2][:, 2048:3072],
                                    in0=WL[ic % 2][:, 2048:3072],
                                    scalar1=lnw_s[:, ic:ic + 1]).then_inc(s_v.h, 1)
            s_v.v += 1
            IV.append(s_v.v)

            @block.tensor
            def _(te, ic=ic, vv=s_v.v):
                te.wait_ge(s_v.h, vv)
                for og in range(6):
                    ins = te.matmul(BPS[og], lnbb[:, ic:ic + 1],
                                    wraw[:, og * 512:(og + 1) * 512],
                                    start=(ic == 0), stop=(ic == 7), **MM)
                ins.then_inc(s_t.h, 1)
            s_t.v += 1
            IT_.append(s_t.v)

        @block.gpsimd
        def _(g, sv=IV[7]):
            g.wait_ge(s_v.h, sv)
            g.dma_start(out=wtf_dram[7 * 128:8 * 128, 0:2048],
                        in_=wst[7 % 2][:, 0:2048]).then_inc(s_g.h, 16)
            g.dma_start(out=wtf_dram[7 * 128:8 * 128, 2048:3072],
                        in_=wst[7 % 2][:, 2048:3072]).then_inc(s_g.h, 16)
        s_g.v += 32

        @block.vector
        def _(v, tv=s_t.v):
            v.wait_ge(s_t.h, tv)
            for og in range(6):
                v.tensor_add(out=beff[:, og * 512:(og + 1) * 512],
                             in0=beff[:, og * 512:(og + 1) * 512], in1=BPS[og])
            v.tensor_copy(out=junk[0:1, 0:1],
                          in_=beff[0:1, 0:1]).then_inc(s_v.h, 1)
        s_v.v += 1
        V_BEFF = s_v.v

        if stop_after == "fold":
            emit_tail([(s_v, s_v.v), (s_g, s_g.v), (s_t, s_t.v), (s_x, s_x.v)])
            return nc

        @block.tensor
        def _(te):
            te.wait_ge(s_v.h, V_BEFF)
            for j in range(24):
                dst = P2[:, j:j + 1] if j < 12 else P3[:, j - 12:j - 11]
                ins = nc.tensor.transpose(dst, beff[0:1, j * 128:(j + 1) * 128],
                                          c_sc[0:1, 0:1])
            ins.then_inc(s_t.h, 1)
        s_t.v += 1
        T_BPC = s_t.v

        @block.vector
        def _(v):
            v.wait_ge(s_t.h, T_BPC)
            v.tensor_copy(out=beff_pc[:, 0:12], in_=P2[:, 0:12])
            v.tensor_copy(out=beff_pc[:, 12:16], in_=P3[:, 0:4])
            v.tensor_copy(out=bv_bf[:], in_=P3[:, 4:12]).then_inc(s_v.h, 1)
        s_v.v += 1
        V_INIT_END = s_v.v

        @block.gpsimd
        def _(g, tb=s_t.v):
            g.wait_ge(s_t.h, tb)
            g.dma_start(out=indf[:], in_=indf_d).then_inc(s_g.h, 16)
            g.dma_start(out=T4[:, 0, 0:1024],
                        in_=lnw_row_d.partition_broadcast(128)).then_inc(s_g.h, 16)
            g.dma_start(out=T4[:, 1, 0:1024],
                        in_=lnb_row_d.partition_broadcast(128)).then_inc(s_g.h, 16)
        s_g.v += 48
        G_INDF = s_g.v

        @block.vector
        def _(v, gi=G_INDF):
            v.wait_ge(s_g.h, gi)
            v.memset(kn2[:].rearrange("a b c -> a (b c)"), 0.0)
            v.memset(qn2[:].rearrange("a b c -> a (b c)"), 0.0).then_inc(s_v.h, 1)
        s_v.v += 1
        V_INDF = s_v.v

        if stop_after == "init":
            emit_tail([(s_v, V_INDF), (s_t, T_BPC), (s_g, s_g.v), (s_x, s_x.v)])
            return nc

        av0, aa0, at0, ax0, ag0 = s_v.v, s_a.v, s_t.v, s_x.v, s_g.v

        V_i1, V_Z, V_E = {}, {}, {}
        V_KMX, V_VEV, V_QMX = {}, {}, {}
        c = av0
        for sg in range(N_SG):
            for j in range(8):
                it = sg * 16 + j
                V_i1[it], V_Z[it], V_E[it] = c + 4, c + 6, c + 7
                c += 7
            for i in range(8):
                it = sg * 16 + 8 + i
                V_i1[it], V_Z[it], V_E[it] = c + 4, c + 6, c + 7
                c += 7
                c += 1
                V_KMX[(sg, i)] = c
            for i in range(8, 16):
                c += 1
                V_KMX[(sg, i)] = c
            for ts in range(16):
                c += 1
                V_VEV[(sg, ts)] = c
            if sg == OWN_SG:
                for i in range(16):
                    c += 1
                    V_QMX[i] = c
        A_V_END = c

        A_1, A_2, A_KEV, A_QEV = {}, {}, {}, {}
        c = aa0
        for sg in range(N_SG):
            for j in range(8):
                it = sg * 16 + j
                A_1[it], A_2[it] = c + 1, c + 2
                c += 2
            for i in range(8):
                it = sg * 16 + 8 + i
                A_1[it], A_2[it] = c + 1, c + 2
                c += 2
                c += 1
                A_KEV[(sg, i)] = c
            for i in range(8, 16):
                c += 1
                A_KEV[(sg, i)] = c
            if sg == OWN_SG:
                for i in range(16):
                    c += 1
                    A_QEV[i] = c
        A_A_END = c

        T_TILE, T_KMM, T_KNRM, T_VMM, T_QMM, T_QNRM = {}, {}, {}, {}, {}, {}
        c = at0
        for sg in range(N_SG):
            for j in range(8):
                c += 1
                T_TILE[sg * 16 + j] = c
            for i in range(8):
                c += 1
                T_KMM[(sg, i)] = c
                if i >= 1:
                    c += 1
                    T_KNRM[(sg, i - 1)] = c
                c += 1
                T_TILE[sg * 16 + 8 + i] = c
            for i in range(8, 16):
                c += 1
                T_KMM[(sg, i)] = c
                c += 1
                T_KNRM[(sg, i - 1)] = c
            c += 1
            T_KNRM[(sg, 15)] = c
            for ts in range(16):
                c += 1
                T_VMM[(sg, ts)] = c
            if sg == OWN_SG:
                for i in range(16):
                    c += 1
                    T_QMM[i] = c
                    if i >= 1:
                        c += 1
                        T_QNRM[i - 1] = c
                c += 1
                T_QNRM[15] = c
        A_T_END = c

        X_TILE, X_WP, X_WPC = {}, {}, {}
        c = ax0
        for sg in range(N_SG):
            c += 16
            X_WP[(sg, 0)] = c
            for j in range(16):
                c += 16
                X_TILE[sg * 16 + j] = c
            for p in range(1, 3 if sg == OWN_SG else 2):
                for icc in range(8):
                    c += 16
                    X_WPC[(sg, p, icc)] = c
                X_WP[(sg, p)] = c
        A_X_END = c

        G_XN, G_K, G_V = {}, {}, {}
        c = ag0
        for sg in range(N_SG):
            if sg == OWN_SG:
                for j in range(8):
                    c += 16
                    G_XN[j] = c
                for i in range(8):
                    c += 16
                    G_XN[8 + i] = c
                    c += 16
                    G_K[(sg, i)] = c
                for i in range(8, 16):
                    c += 16
                    G_K[(sg, i)] = c
            else:
                for i in range(16):
                    c += 16
                    G_K[(sg, i)] = c
            for ts in range(16):
                c += 16
                G_V[(sg, ts)] = c
        A_G_END = c

        @block.sync
        def _(sy):
            def wload(sy, col, wait, split=False):
                sy.wait_ge(s_t.h, wait)
                wsrc = wtf_dram.rearrange("(c p) o -> p c o", p=128)
                if not split:
                    sy.dma_start(out=WAr[:],
                                 in_=wsrc[:, :, col:col + 1024]
                                 ).then_inc(s_x.h, 16)
                    return
                for icc in range(8):
                    sy.dma_start(out=WAr[:, icc, :],
                                 in_=wsrc[:, icc, col:col + 1024]
                                 ).then_inc(s_x.h, 16)

            for sg in range(N_SG):
                wload(sy, 1024, T_BPC if sg == 0 else T_VMM[(sg - 1, 15)])
                for j in range(16):
                    it = sg * 16 + j
                    if it == 2:
                        sy.wait_ge(s_v.h, V_INIT_END)
                    if it >= 4:
                        sy.wait_ge(s_v.h, V_Z[it - 4])
                    sy.dma_start(out=xin[it % 4],
                                 in_=x_full[it * 128:(it + 1) * 128, :]
                                 ).then_inc(s_x.h, 16)
                wload(sy, 2048, T_KMM[(sg, 15)], split=True)
                if sg == OWN_SG:
                    wload(sy, 0, T_VMM[(sg, 15)], split=True)

        @block.scalar
        def _(sc):
            def tile_act(sg, j):
                it = sg * 16 + j
                sl = (0, 1, 4, 5)[it % 4]
                sc.wait_ge(s_x.h, X_TILE[it])
                if it >= 4:
                    sc.wait_ge(s_v.h, V_Z[it - 4])
                sc.activation(out=junk[:, 0:1024], in_=xin[it % 4],
                              func=AF.Identity, accum_out=stat[:, sl, 0:1])
                sc.activation(out=junk[:, 0:1024], in_=xin[it % 4],
                              func=AF.Square,
                              accum_out=stat[:, sl, 1:2]).then_inc(s_a.h, 1)
                sc.wait_ge(s_v.h, V_i1[it])
                sc.activation(out=stat[:, sl, 7:8], in_=stat[:, sl, 5:6],
                              func=AF.Ln)
                ins = sc.activation(out=stat[:, sl, 6:7],
                                    in_=stat[:, sl, 7:8],
                                    func=AF.Exp, scale=-0.5)
                if sg == OWN_SG:
                    sc.activation(out=stat[:, sl, 9:10], in_=stat[:, sl, 4:5],
                                  func=AF.Ln)
                    ins = sc.activation(out=stat[:, sl, 8:9],
                                        in_=stat[:, sl, 9:10],
                                        func=AF.Exp, scale=-0.5)
                ins.then_inc(s_a.h, 1)

            def kev_act(sg, i):
                h4, oc = i // 8, i % 8
                if i >= 2:
                    sc.wait_ge(s_g.h, G_K[(sg, i - 2)])
                elif sg > 0:
                    sc.wait_ge(s_g.h, G_K[(sg - 1, 14 + i)])
                else:
                    sc.wait_ge(s_g.h, ag0)
                sc.wait_ge(s_t.h, T_KMM[(sg, i)])
                if i >= 2:
                    sc.wait_ge(s_t.h, T_KNRM[(sg, i - 2)])
                elif sg > 0:
                    sc.wait_ge(s_t.h, T_KNRM[(sg - 1, 14 + i)])
                bks = [P0[:, 0:512], P0[:, 512:1024]] if i % 2 == 0 else \
                    [P1[:, 0:512], P1[:, 512:1024]]
                for t2 in range(2):
                    sc.activation(out=kstage2[i % 2][:, t2 * 512:(t2 + 1) * 512],
                                  in_=bks[t2], func=AF.Identity,
                                  bias=beff_pc[:, 8 + oc:9 + oc])
                sc.activation(out=ksq[:, (i % 2) * 1024:(i % 2 + 1) * 1024],
                              in_=kstage2[i % 2][:],
                              func=AF.Square).then_inc(s_a.h, 1)

            sc.wait_ge(s_v.h, V_INDF)
            for sg in range(N_SG):
                for j in range(8):
                    tile_act(sg, j)
                for i in range(8):
                    tile_act(sg, 8 + i)
                    kev_act(sg, i)
                for i in range(8, 16):
                    kev_act(sg, i)
                if sg == OWN_SG:
                    for i in range(16):
                        h4, oc = i // 8, i % 8
                        sc.wait_ge(s_t.h, T_QMM[i])
                        if i >= 2:
                            sc.wait_ge(s_t.h, T_QNRM[i - 2])
                        else:
                            sc.wait_ge(s_t.h, T_KNRM[(sg, 14 + i)])
                        bks = [P0[:, 0:512], P0[:, 512:1024]] if i % 2 == 0 else \
                            [P1[:, 0:512], P1[:, 512:1024]]
                        for t2 in range(2):
                            sc.activation(
                                out=T3[:, oc, h4 * 1024 + t2 * 512:
                                       h4 * 1024 + (t2 + 1) * 512],
                                in_=bks[t2], func=AF.Identity,
                                bias=beff_pc[:, oc:oc + 1])
                        sc.activation(out=ksq[:, (i % 2) * 1024:(i % 2 + 1) * 1024],
                                      in_=T3[:, oc, h4 * 1024:(h4 + 1) * 1024],
                                      func=AF.Square).then_inc(s_a.h, 1)

        @block.vector
        def _(v):
            def tile_dve(sg, j):
                    it = sg * 16 + j
                    sl = (0, 1, 4, 5)[it % 4]
                    zsl = it % 2
                    b = V_i1[it] - 4
                    v.wait_ge(s_a.h, A_1[it])
                    if it >= 4:
                        v.wait_ge(s_a.h, A_2[it - 4])
                    v.tensor_scalar_mul(out=stat[:, sl, 2:3], in0=stat[:, sl, 0:1],
                                        scalar1=1.0 / 1024).then_inc(s_v.h, 1)
                    v.wait_ge(s_v.h, b + 1)
                    v.tensor_scalar(out=stat[:, sl, 3:4], in0=stat[:, sl, 2:3],
                                    scalar1=stat[:, sl, 2:3], scalar2=-EPS,
                                    op0=ALU.mult, op1=ALU.add).then_inc(s_v.h, 1)
                    v.wait_ge(s_v.h, b + 2)
                    v.scalar_tensor_tensor(out=stat[:, sl, 4:5],
                                           in0=stat[:, sl, 1:2], scalar=1.0 / 1024,
                                           in1=stat[:, sl, 3:4], op0=ALU.mult,
                                           op1=ALU.subtract).then_inc(s_v.h, 1)
                    v.wait_ge(s_v.h, b + 3)
                    v.tensor_scalar(out=stat[:, sl, 5:6], in0=stat[:, sl, 4:5],
                                    scalar1=1.0 + EPS, scalar2=-EPS,
                                    op0=ALU.mult, op1=ALU.add).then_inc(s_v.h, 1)
                    v.wait_ge(s_a.h, A_2[it])
                    ins = v.scalar_tensor_tensor(out=stat[:, sl, 10:11],
                                                 in0=stat[:, sl, 2:3], scalar=-1.0,
                                                 in1=stat[:, sl, 6:7],
                                                 op0=ALU.mult, op1=ALU.mult)
                    if sg == OWN_SG:
                        ins = v.scalar_tensor_tensor(out=stat[:, sl, 11:12],
                                                     in0=stat[:, sl, 2:3],
                                                     scalar=-1.0,
                                                     in1=stat[:, sl, 8:9],
                                                     op0=ALU.mult, op1=ALU.mult)
                    ins.then_inc(s_v.h, 1)
                    v.wait_ge(s_v.h, b + 5)
                    if sg == OWN_SG and j >= 2:
                        v.wait_ge(s_g.h, G_XN[j - 2])
                    v.tensor_scalar(out=zb_t[zsl][:, 0:1024], in0=xin[it % 4],
                                    scalar1=stat[:, sl, 6:7],
                                    scalar2=stat[:, sl, 10:11],
                                    op0=ALU.mult, op1=ALU.add)
                    if sg == OWN_SG:
                        v.tensor_scalar(out=xo_t[zsl], in0=xin[it % 4],
                                        scalar1=stat[:, sl, 8:9],
                                        scalar2=stat[:, sl, 11:12],
                                        op0=ALU.mult, op1=ALU.add)
                        v.tensor_copy(out=T2[:, j, :], in_=zb_t[zsl][:, 0:1024])
                    v.tensor_copy(out=junk[0:1, 0:1],
                                  in_=zb_t[zsl][0:1, 0:1]).then_inc(s_v.h, 1)
                    v.wait_ge(s_t.h, T_TILE[it])
                    if j == 0 and sg >= 1:
                        last = T_QMM[15] if sg - 1 == OWN_SG else \
                            T_VMM[(sg - 1, 15)]
                        v.wait_ge(s_t.h, last)
                    ps = P4 if it % 2 == 0 else P5
                    v.tensor_copy(out=T1[:, :, j * 128:(j + 1) * 128],
                                  in_=ps[:].rearrange("p (c t) -> p c t", c=8)
                                  ).then_inc(s_v.h, 1)

            def kmax_dve(sg, i):
                h4 = i // 8
                v.wait_ge(s_t.h, T_KNRM[(sg, i)])
                v.tensor_tensor(out=kn2[:, 2 * h4, :], in0=kn2[:, 2 * h4, :],
                                in1=P2[0:16, :], op=ALU.max)
                v.tensor_tensor(out=kn2[:, 2 * h4 + 1, :],
                                in0=kn2[:, 2 * h4 + 1, :], in1=P3[0:16, :],
                                op=ALU.max).then_inc(s_v.h, 1)

            for sg in range(N_SG):
                for j in range(8):
                    tile_dve(sg, j)
                for i in range(8):
                    tile_dve(sg, 8 + i)
                    kmax_dve(sg, i)
                for i in range(8, 16):
                    kmax_dve(sg, i)
                for ts in range(16):
                    v.wait_ge(s_t.h, T_VMM[(sg, ts)])
                    if ts >= 2:
                        v.wait_ge(s_g.h, G_V[(sg, ts - 2)])
                    elif sg > 0:
                        v.wait_ge(s_g.h, G_V[(sg - 1, 14 + ts)])
                    b0 = P0 if ts % 2 == 0 else P1
                    v.tensor_copy(out=vstage[ts % 2][:, 0:1024],
                                  in_=b0[:]).then_inc(s_v.h, 1)
                if sg == OWN_SG:
                    for i in range(16):
                        h4 = i // 8
                        v.wait_ge(s_t.h, T_QNRM[i])
                        v.tensor_tensor(out=qn2[:, 2 * h4, :],
                                        in0=qn2[:, 2 * h4, :], in1=P2[0:16, :],
                                        op=ALU.max)
                        v.tensor_tensor(out=qn2[:, 2 * h4 + 1, :],
                                        in0=qn2[:, 2 * h4 + 1, :],
                                        in1=P3[0:16, :],
                                        op=ALU.max).then_inc(s_v.h, 1)
                        if i == 0:
                            v.wait_ge(s_g.h, G_INDF)
                        v.tensor_mul(out=T2[:, i, :], in0=T2[:, i, :],
                                     in1=T4[:, 0, 0:1024])
                        v.tensor_add(out=T2[:, i, :], in0=T2[:, i, :],
                                     in1=T4[:, 1, 0:1024])
                    for icc in range(8):
                        v.tensor_scalar(out=T1[:, icc, :], in0=T1[:, icc, :],
                                        scalar1=lnw_s[:, icc:icc + 1],
                                        scalar2=lnb_s[:, icc:icc + 1],
                                        op0=ALU.mult, op1=ALU.add)

        @block.tensor
        def _(te):
            def tr_pe(sg, j):
                it = sg * 16 + j
                sl = it % 2
                te.wait_ge(s_v.h, V_Z[it])
                if it >= 2:
                    te.wait_ge(s_v.h, V_E[it - 2])
                ps = P4 if it % 2 == 0 else P5
                for cch in range(8):
                    ins = nc.tensor.transpose(ps[:, cch * 128:(cch + 1) * 128],
                                              zb_t[sl][:, cch * 128:(cch + 1) * 128],
                                              identbf[:])
                ins.then_inc(s_t.h, 1)

            def kg_pe(sg, i):
                h4, oc = i // 8, i % 8
                if i == 0:
                    te.wait_ge(s_v.h, V_E[sg * 16 + 7])
                elif i == 8:
                    te.wait_ge(s_v.h, V_E[sg * 16 + 15])
                if i >= 2:
                    te.wait_ge(s_a.h, A_KEV[(sg, i - 2)])
                elif sg > 0:
                    te.wait_ge(s_v.h, V_VEV[(sg - 1, 14 + i)])
                bks = [P0[:, 0:512], P0[:, 512:1024]] if i % 2 == 0 else \
                    [P1[:, 0:512], P1[:, 512:1024]]
                for icc in range(8):
                    for t2 in range(2):
                        ins = te.matmul(
                            bks[t2], WAr[:, icc, oc * 128:(oc + 1) * 128],
                            T1[:, icc, h4 * 1024 + t2 * 512:
                               h4 * 1024 + (t2 + 1) * 512],
                            start=(icc == 0), stop=(icc == 7), **MM)
                ins.then_inc(s_t.h, 1)

            def knorm_pe(sg, m):
                ocm = m % 8
                te.wait_ge(s_a.h, A_KEV[(sg, m)])
                if m >= 1:
                    te.wait_ge(s_v.h, V_KMX[(sg, m - 1)])
                elif sg > 0:
                    te.wait_ge(s_v.h, V_KMX[(sg - 1, 15)])
                else:
                    te.wait_ge(s_v.h, V_INIT_END)
                hb = (m % 2) * 1024
                ins = te.matmul(P2[0:16, :], indf[:, ocm, :],
                                ksq[:, hb:hb + 512],
                                start=True, stop=True, **MM)
                ins = te.matmul(P3[0:16, :], indf[:, ocm, :],
                                ksq[:, hb + 512:hb + 1024],
                                start=True, stop=True, **MM)
                ins.then_inc(s_t.h, 1)

            for sg in range(N_SG):
                for j in range(8):
                    tr_pe(sg, j)
                te.wait_ge(s_x.h, X_WP[(sg, 0)])
                for i in range(8):
                    kg_pe(sg, i)
                    if i >= 1:
                        knorm_pe(sg, i - 1)
                    tr_pe(sg, 8 + i)
                for i in range(8, 16):
                    kg_pe(sg, i)
                    knorm_pe(sg, i - 1)
                knorm_pe(sg, 15)
                te.wait_ge(s_a.h, A_KEV[(sg, 15)])
                for ts in range(16):
                    if ts >= 2:
                        te.wait_ge(s_v.h, V_VEV[(sg, ts - 2)])
                    elif sg > 0:
                        te.wait_ge(s_v.h, V_VEV[(sg - 1, 14 + ts)])
                    b0 = P0 if ts % 2 == 0 else P1
                    for icc in range(8):
                        if ts == 0:
                            te.wait_ge(s_x.h, X_WPC[(sg, 1, icc)])
                        ins = te.matmul(b0[:, 0:512],
                                        T1[:, icc, ts * 128:(ts + 1) * 128],
                                        WAr[:, icc, 0:512],
                                        start=(icc == 0), stop=(icc == 7), **MM)
                        ins = te.matmul(b0[:, 512:1024],
                                        T1[:, icc, ts * 128:(ts + 1) * 128],
                                        WAr[:, icc, 512:1024],
                                        start=(icc == 0), stop=(icc == 7), **MM)
                    ins.then_inc(s_t.h, 1)
                if sg == OWN_SG:
                    te.wait_ge(s_v.h, V_VEV[(sg, 15)])
                    for i in range(16):
                        h4, oc = i // 8, i % 8
                        if i >= 2:
                            te.wait_ge(s_a.h, A_QEV[i - 2])
                        bks = [P0[:, 0:512], P0[:, 512:1024]] if i % 2 == 0 \
                            else [P1[:, 0:512], P1[:, 512:1024]]
                        for icc in range(8):
                            if i == 0:
                                te.wait_ge(s_x.h, X_WPC[(sg, 2, icc)])
                            for t2 in range(2):
                                ins = te.matmul(
                                    bks[t2],
                                    WAr[:, icc, oc * 128:(oc + 1) * 128],
                                    T1[:, icc, h4 * 1024 + t2 * 512:
                                       h4 * 1024 + (t2 + 1) * 512],
                                    start=(icc == 0), stop=(icc == 7), **MM)
                        ins.then_inc(s_t.h, 1)
                        if i >= 1:
                            m = i - 1
                            ocm = m % 8
                            te.wait_ge(s_a.h, A_QEV[m])
                            if m >= 1:
                                te.wait_ge(s_v.h, V_QMX[m - 1])
                            else:
                                te.wait_ge(s_v.h, V_KMX[(sg, 15)])
                            hb = (m % 2) * 1024
                            ins = te.matmul(P2[0:16, :], indf[:, ocm, :],
                                            ksq[:, hb:hb + 512],
                                            start=True, stop=True, **MM)
                            ins = te.matmul(P3[0:16, :], indf[:, ocm, :],
                                            ksq[:, hb + 512:hb + 1024],
                                            start=True, stop=True, **MM)
                            ins.then_inc(s_t.h, 1)
                    m = 15
                    te.wait_ge(s_a.h, A_QEV[m])
                    te.wait_ge(s_v.h, V_QMX[m - 1])
                    hb = (m % 2) * 1024
                    ins = te.matmul(P2[0:16, :], indf[:, m % 8, :],
                                    ksq[:, hb:hb + 512],
                                    start=True, stop=True, **MM)
                    ins = te.matmul(P3[0:16, :], indf[:, m % 8, :],
                                    ksq[:, hb + 512:hb + 1024],
                                    start=True, stop=True, **MM)
                    ins.then_inc(s_t.h, 1)

        @block.gpsimd
        def _(g):
            def xn_spill(sg, j):
                g.wait_ge(s_v.h, V_Z[sg * 16 + j])
                g.dma_start(out=xn_dram[j * 128:(j + 1) * 128, :],
                            in_=xo_t[(sg * 16 + j) % 2]).then_inc(s_g.h, 16)

            def k_spill(sg, i):
                h4, oc = i // 8, i % 8
                g.wait_ge(s_a.h, A_KEV[(sg, i)])
                g.dma_start(out=kT_dram[oc * 128:(oc + 1) * 128,
                                        sg * SG + h4 * 1024:
                                        sg * SG + (h4 + 1) * 1024],
                            in_=kstage2[i % 2][:]).then_inc(s_g.h, 16)

            for sg in range(N_SG):
                if sg == OWN_SG:
                    for j in range(8):
                        xn_spill(sg, j)
                    for i in range(8):
                        xn_spill(sg, 8 + i)
                        k_spill(sg, i)
                    for i in range(8, 16):
                        k_spill(sg, i)
                else:
                    for i in range(16):
                        k_spill(sg, i)
                for ts in range(16):
                    g.wait_ge(s_v.h, V_VEV[(sg, ts)])
                    g.dma_start(out=v_dram[sg * SG + ts * 128:
                                           sg * SG + (ts + 1) * 128, :],
                                in_=vstage[ts % 2][:, 0:1024]).then_inc(s_g.h, 16)
        s_x.v = A_X_END
        s_g.v = A_G_END
        s_v.v = A_V_END
        s_a.v = A_A_END
        s_t.v = A_T_END

        if debug:
            @block.gpsimd
            def _(g):
                g.wait_ge(s_v.h, A_V_END)
                g.dma_start(out=dbg["zT"], in_=T1[:]).then_inc(s_g.h, 16)
                g.dma_start(out=dbg["qT"], in_=T3[:]).then_inc(s_g.h, 16)
            s_g.v += 32
        G_DBG_A = s_g.v

        @block.vector
        def _(v, v0=s_v.v):
            v.reduce_max(out=nrm[:, 0:4], in_=kn2[:], axis=AX).then_inc(s_v.h, 1)
            v.wait_ge(s_v.h, v0 + 1)
            v.reduce_max(out=nrm[:, 0:1], in_=nrm[:, 0:4],
                         axis=AX).then_inc(s_v.h, 1)
            v.reduce_max(out=nrm[:, 4:8], in_=qn2[:], axis=AX).then_inc(s_v.h, 1)
            v.wait_ge(s_v.h, v0 + 3)
            v.reduce_max(out=nrm[:, 1:2], in_=nrm[:, 4:8],
                         axis=AX).then_inc(s_v.h, 1)
            v.wait_ge(s_v.h, v0 + 4)
            v.tensor_scalar(out=nrm[:, 2:3], in0=nrm[:, 0:1],
                            scalar1=nrm[:, 1:2], scalar2=1e-30,
                            op0=ALU.mult, op1=ALU.add).then_inc(s_v.h, 1)
        s_v.v += 5
        V_NRM = s_v.v

        @block.scalar
        def _(sc):
            sc.wait_ge(s_v.h, V_NRM)
            sc.activation(out=nrm[:, 3:4], in_=nrm[:, 2:3], func=AF.Ln)
            sc.activation(out=nrm[:, 4:5], in_=nrm[:, 3:4], func=AF.Exp,
                          scale=0.5).then_inc(s_a.h, 1)
        s_a.v += 1
        A_C1 = s_a.v

        @block.gpsimd
        def _(g):
            g.wait_ge(s_a.h, A_C1)
            g.dma_start(out=c_sc[0:1, 0:16], in_=nrm[0:16, 4:5],
                        allow_slow_non_contiguous=True).then_inc(s_g.h, 16)
        s_g.v += 16
        G_C1 = s_g.v

        @block.vector
        def _(v, v0=s_v.v):
            v.wait_ge(s_g.h, G_C1)
            v.memset(ones_f[:], 1.0).then_inc(s_v.h, 1)
        s_v.v += 1
        V_C2 = s_v.v

        @block.tensor
        def _(te):
            te.wait_ge(s_v.h, V_C2)
            te.matmul(P3[:, 0:16], ones_f[0:1, :], c_sc[0:1, 0:16],
                      start=True, stop=True, **MM).then_inc(s_t.h, 1)
        s_t.v += 1
        T_C2 = s_t.v

        @block.vector
        def _(v, v0=s_v.v):
            v.wait_ge(s_t.h, T_C2)
            v.tensor_scalar(out=negc8h, in0=P3[:, 0:16], scalar1=-0.125,
                            scalar2=60.0, op0=ALU.mult,
                            op1=ALU.add).then_inc(s_v.h, 1)
            v.wait_ge(s_v.h, v0 + 1)
            hb = stat[:, 0, 0:16].rearrange("p (a b) -> p a b", b=2)
            v.tensor_tensor(out=negc8p, in0=hb[:, :, 0:1], in1=hb[:, :, 1:2],
                            op=ALU.min).then_inc(s_v.h, 1)
        s_v.v += 2
        V_C3 = s_v.v

        if debug:
            @block.gpsimd
            def _(g):
                g.wait_ge(s_v.h, V_C3)
                g.dma_start(out=dbg["kn2"], in_=kn2[:]).then_inc(s_g.h, 16)
                g.dma_start(out=dbg["qn2"], in_=qn2[:]).then_inc(s_g.h, 16)
                g.dma_start(out=dbg["nrm"], in_=nrm[:]).then_inc(s_g.h, 16)
                g.dma_start(out=dbg["c_sc"], in_=c_sc[:]).then_inc(s_g.h, 16)
                g.dma_start(out=dbg["indf"], in_=indf[:]).then_inc(s_g.h, 16)
            s_g.v += 16 * 5

        if stop_after == "A":
            emit_tail([(s_v, s_v.v), (s_t, s_t.v), (s_a, s_a.v),
                       (s_g, s_g.v), (s_x, s_x.v)])
            return nc

        @block.gpsimd
        def _(g, tend=A_T_END):
            g.wait_ge(s_t.h, tend)
            g.dma_start(out=lw0T[:], in_=w0T_d).then_inc(s_g.h, 16)
            g.dma_start(out=lm1[:], in_=w1_d).then_inc(s_g.h, 16)
            g.dma_start(out=lw1T[:], in_=w1T_d).then_inc(s_g.h, 16)
            g.dma_start(out=lw2T[:], in_=w2T_d).then_inc(s_g.h, 16)
        s_g.v += 16 * 4
        G_LACTW = s_g.v

        @block.vector
        def _(v, gdbg=G_LACTW):
            v.wait_ge(s_g.h, gdbg)
            v.tensor_scalar_mul(out=lm1[:].rearrange("p a b -> p (a b)"),
                                in0=lm1[:].rearrange("p a b -> p (a b)"),
                                scalar1=-1.0)
            v.tensor_copy(out=junk[0:1, 0:1],
                          in_=T2[0:1, 0, 0:1]).then_inc(s_v.h, 1)
        s_v.v += 1
        V_FOLD = s_v.v

        if debug:
            @block.gpsimd
            def _(g):
                g.wait_ge(s_v.h, V_FOLD)
                g.dma_start(out=dbg["xn2tok"], in_=T2[:]).then_inc(s_g.h, 16)
            s_g.v += 16

        g_gate, g_g2, g_dh, g_sig, g_hid, g_dg, g_u = gsc
        for u in range(8):
            n, h = u // 4, u % 4
            vbase = s_v.v
            abase = s_a.v
            tbase = s_t.v

            @block.tensor
            def _(te, n=n, h=h, vbase=vbase, abase=abase):
                te.wait_ge(s_v.h, vbase)
                for j in range(8):
                    if j >= 2:
                        jm = j - 2
                        te.wait_ge(s_v.h, vbase + jm + (1 if jm < 4 else 2))
                        te.wait_ge(s_a.h, abase + jm + 1)
                    bg = P0[:, 0:256] if j % 2 == 0 else P0[:, 512:768]
                    b2 = P1[:, 0:256] if j % 2 == 0 else P1[:, 512:768]
                    bd = P2[:, 0:256] if j % 2 == 0 else P3[:, 0:256]
                    for cc in range(2):
                        kv = T1[:, 2 * h + cc,
                                n * 1024 + j * 128:n * 1024 + (j + 1) * 128]
                        ins = te.matmul(bg, kv, lw0T[:, 2 * h + cc, :],
                                        start=(cc == 0), stop=(cc == 1), **MM)
                        ins = te.matmul(b2, kv, lw2T[:, 2 * h + cc, :],
                                        start=(cc == 0), stop=(cc == 1), **MM)
                        ins = te.matmul(bd, kv, lm1[:, 2 * h + cc, :],
                                        start=(cc == 0), stop=(cc == 1), **MM)
                    ins.then_inc(s_t.h, 1)
            s_t.v += 8

            @block.scalar
            def _(sc, tbase=tbase):
                for j in range(8):
                    sc.wait_ge(s_t.h, tbase + j + 1)
                    bg = P0[:, 0:256] if j % 2 == 0 else P0[:, 512:768]
                    sc.activation(out=g_sig[:, j * 256:(j + 1) * 256], in_=bg,
                                  func=AF.Sigmoid).then_inc(s_a.h, 1)
            s_a.v += 8
            A_S1 = s_a.v

            @block.vector
            def _(v, tbase=tbase, A_S1=A_S1, abase=abase):
                for hf in range(2):
                    for j in range(4 * hf, 4 * hf + 4):
                        v.wait_ge(s_t.h, tbase + j + 1)
                        v.wait_ge(s_a.h, abase + j + 1)
                        bg = P0[:, 0:256] if j % 2 == 0 else P0[:, 512:768]
                        b2 = P1[:, 0:256] if j % 2 == 0 else P1[:, 512:768]
                        bd = P2[:, 0:256] if j % 2 == 0 else P3[:, 0:256]
                        v.tensor_copy(out=g_gate[:, j * 256:(j + 1) * 256], in_=bg)
                        v.tensor_copy(out=g_g2[:, j * 256:(j + 1) * 256], in_=b2)
                        v.tensor_copy(out=g_dh[:, j * 256:(j + 1) * 256],
                                      in_=bd).then_inc(s_v.h, 1)
                    cs = slice(hf * 1024, hf * 1024 + 1024)
                    gs, gg, g2_, gd, gu, gh, gdg = (g_sig[:, cs], g_gate[:, cs],
                                                    g_g2[:, cs], g_dh[:, cs],
                                                    g_u[:, cs], g_hid[:, cs],
                                                    g_dg[:, cs])
                    v.tensor_mul(out=gu, in0=gg, in1=gs)
                    v.tensor_mul(out=gh, in0=gu, in1=g2_)
                    v.tensor_mul(out=gdg, in0=gd, in1=gu)
                    v.tensor_mul(out=gd, in0=gd, in1=g2_)
                    v.tensor_scalar(out=gg, in0=gs, scalar1=-1.0, scalar2=1.0,
                                    op0=ALU.mult, op1=ALU.add)
                    v.tensor_mul(out=gg, in0=gu, in1=gg)
                    v.tensor_add(out=gg, in0=gs, in1=gg)
                    v.tensor_mul(out=gu, in0=gd, in1=gg).then_inc(s_v.h, 1)
            s_v.v += 10
            V_EW0 = s_v.v - 5
            V_EW1 = s_v.v

            @block.tensor
            def _(te, n=n, h=h, vv=V_EW1):
                te.wait_ge(s_v.h, vv)
                for dh_ in range(2):
                    bp = P0[:, dh_ * 512 + 256:dh_ * 512 + 512]
                    for j in range(8):
                        kv = T2[:, 8 * n + j, h * 256 + dh_ * 128:
                                h * 256 + (dh_ + 1) * 128]
                        ins = te.matmul(bp, kv, g_u[:, j * 256:(j + 1) * 256],
                                        start=(j == 0), stop=(j == 7), **MM)
                for dh_ in range(2):
                    bp = P1[:, dh_ * 512 + 256:dh_ * 512 + 512]
                    for j in range(8):
                        kv = T2[:, 8 * n + j, h * 256 + dh_ * 128:
                                h * 256 + (dh_ + 1) * 128]
                        ins = te.matmul(bp, kv, g_dg[:, j * 256:(j + 1) * 256],
                                        start=(j == 0), stop=(j == 7), **MM)
                for eh in range(2):
                    bp = P2[:, 256:512] if eh == 0 else P3[:, 256:512]
                    for j in range(8):
                        hid = g_hid[:, j * 256 + eh * 128:j * 256 + (eh + 1) * 128]
                        kv = T2[:, 8 * n + j, h * 256:(h + 1) * 256]
                        ins = te.matmul(bp, hid, kv,
                                        start=(j == 0), stop=(j == 7), **MM)
                ins.then_inc(s_t.h, 1)
            s_t.v += 1
            T_S2 = s_t.v

            @block.vector
            def _(v, h=h, tv=T_S2):
                v.wait_ge(s_t.h, tv)
                for cc in range(2):
                    v.tensor_sub(out=wn0[:, cc, :], in0=lw0T[:, 2 * h + cc, :],
                                 in1=P0[:, cc * 512 + 256:cc * 512 + 512])
                    v.tensor_sub(out=wn2[:, cc, :], in0=lw2T[:, 2 * h + cc, :],
                                 in1=P1[:, cc * 512 + 256:cc * 512 + 512])
                    v.tensor_add(out=wn1[:, cc, :], in0=lw1T[:, 2 * h + cc, :],
                                 in1=(P2 if cc == 0 else P3)[:, 256:512])
                v.tensor_copy(out=junk[0:1, 0:1],
                              in_=wn0[0:1, 0, 0:1]).then_inc(s_v.h, 1)
            s_v.v += 1
            V_WN = s_v.v

            rnd_v = []
            for eh in range(2):
                @block.tensor
                def _(te, n=n, h=h, eh=eh, wv=V_WN, rv=(rnd_v[0] if eh else None)):
                    te.wait_ge(s_v.h, wv if eh == 0 else rv)
                    for th in range(2):
                        bq = P0[:, th * 512:(th + 1) * 512]
                        b2 = P1[:, th * 512:(th + 1) * 512]
                        for cc in range(2):
                            kv = T1[:, 2 * h + cc,
                                    n * 1024 + th * 512:n * 1024 + (th + 1) * 512]
                            ins = te.matmul(bq, wn0[:, cc, eh * 128:(eh + 1) * 128],
                                            kv, start=(cc == 0), stop=(cc == 1),
                                            **MM)
                            ins = te.matmul(b2, wn2[:, cc, eh * 128:(eh + 1) * 128],
                                            kv, start=(cc == 0), stop=(cc == 1),
                                            **MM)
                    ins.then_inc(s_t.h, 1)
                s_t.v += 1
                T_R = s_t.v

                @block.scalar
                def _(sc, eh=eh, tv=T_R):
                    sc.wait_ge(s_t.h, tv)
                    for th in range(2):
                        sc.activation(out=g_sig[:, eh * 1024 + th * 512:
                                              eh * 1024 + (th + 1) * 512],
                                      in_=P0[:, th * 512:(th + 1) * 512],
                                      func=AF.Sigmoid).then_inc(s_a.h, 1)
                s_a.v += 2
                A_R = s_a.v

                @block.vector
                def _(v, eh=eh, tv=T_R, av=A_R):
                    v.wait_ge(s_t.h, tv)
                    sl_ = slice(eh * 1024, (eh + 1) * 1024)
                    v.tensor_copy(out=g_gate[:, sl_], in_=P0[:])
                    v.tensor_copy(out=g_g2[:, sl_], in_=P1[:])
                    v.wait_ge(s_a.h, av)
                    v.tensor_mul(out=g_gate[:, sl_], in0=g_gate[:, sl_],
                                 in1=g_sig[:, sl_])
                    v.tensor_mul(out=g_u[:, sl_], in0=g_gate[:, sl_],
                                 in1=g_g2[:, sl_]).then_inc(s_v.h, 1)
                s_v.v += 1
                rnd_v.append(s_v.v)

            @block.tensor
            def _(te, n=n, h=h, vv=s_v.v, tb=s_t.v, vb=s_v.v):
                te.wait_ge(s_v.h, vv)
                idx = 0
                for dh_ in range(2):
                    for th in range(2):
                        if idx >= 2:
                            te.wait_ge(s_v.h, vb + (idx - 2) + 1)
                        bp = P2 if idx % 2 == 0 else P3
                        for eh in range(2):
                            ins = te.matmul(bp[:],
                                            wn1[:, eh, dh_ * 128:(dh_ + 1) * 128],
                                            g_u[:, eh * 1024 + th * 512:
                                                eh * 1024 + (th + 1) * 512],
                                            start=(eh == 0), stop=(eh == 1), **MM)
                        ins.then_inc(s_t.h, 1)
                        idx += 1
            s_t.v += 4

            @block.vector
            def _(v, n=n, h=h, t0=s_t.v - 4):
                idx = 0
                for dh_ in range(2):
                    for th in range(2):
                        v.wait_ge(s_t.h, t0 + idx + 1)
                        bp = P2 if idx % 2 == 0 else P3
                        v.tensor_copy(
                            out=T4[:, 2 * h + dh_,
                                   n * 1024 + th * 512:n * 1024 + (th + 1) * 512],
                            in_=bp[:]).then_inc(s_v.h, 1)
                        idx += 1
            s_v.v += 4
        V_B_END = s_v.v
        T_B_END = s_t.v

        if debug:
            @block.gpsimd
            def _(g):
                g.wait_ge(s_v.h, V_B_END)
                g.dma_start(out=dbg["lactT"], in_=T4[:]).then_inc(s_g.h, 16)
            s_g.v += 16

        if stop_after == "B":
            emit_tail([(s_v, s_v.v), (s_t, s_t.v), (s_a, s_a.v),
                       (s_g, s_g.v), (s_x, s_x.v)])
            return nc

        cg0, ct0 = s_g.v, s_t.v
        kv_after = lambda g: 16 * (g + 1) + 16 * (g // 4 + 1)
        A_k = lambda g: 64 * (g // 4) + 16 * (g // 4 + 1)

        @block.vector
        def _(v, vb=V_B_END, gb=s_g.v):
            v.wait_ge(s_g.h, gb)
            for sl_ in range(8):
                v.memset(vtb[sl_][:, 64:65], 1.0)
                v.memset(vtb[sl_][:, 129:130], 1.0)
            v.tensor_copy(out=junk[0:1, 0:1],
                          in_=vtb[0][0:1, 0:1]).then_inc(s_v.h, 1)
        s_v.v += 1
        V_ONES = s_v.v
        cv0 = s_v.v

        V_QSC, V_QLD, V_QTE = {}, {}, {}
        c = cv0
        for u in range(32):
            if u < N_QB:
                V_QSC[u] = c + 3
                V_QLD[u] = c + 4
                c += 4
            if u >= 2 and (u - 2) < N_QB:
                c += 1
                V_QTE[u - 2] = c
        C_V_END = c

        T_QTR = {}
        c = ct0
        for i in range(N_QB):
            c += 1
            T_QTR[i] = c
        C_T_END = c

        G_ZQ = {}
        c = cg0
        for u in range(32):
            c += 32
            if u < N_QB:
                c += 16
            G_ZQ[u] = c
        C_G_END = c

        @block.sync
        def _(sy):
            sy.wait_ge(s_v.h, V_ONES)
            for g in range(TOT_ITER):
                kc = g % 64
                hp = g // (N_EP * N_KC)
                if kc % 4 == 0:
                    if g >= 16:
                        sy.wait_ge(s_qk.h, g - 12)
                    sy.dma_start(out=ktb[(g // 4) % 4],
                                 in_=kT_dram[hp * 128:(hp + 1) * 128,
                                             kc * 128:kc * 128 + 512]
                                 ).then_inc(s_kv.h, 16)
                if g >= 8:
                    sy.wait_ge(s_pv.h, g - 7)
                sy.dma_start(out=vtb[g % 8][:, 0:130]
                             .rearrange("p (h c) -> p h c", h=2)[:, :, 0:64],
                             in_=v_dram[kc * 128:(kc + 1) * 128,
                                        hp * 128:(hp + 1) * 128]
                             .rearrange("k (h c) -> k h c", h=2)
                             ).then_inc(s_kv.h, 16)
        s_kv.v = 16 * TOT_ITER + 16 * (TOT_ITER // 4)

        @block.tensor
        def _(te):
            te.wait_ge(s_v.h, V_ONES)
            for g in range(TOT_ITER):
                kc = g % 64
                u = g // 64
                hp = u // N_EP
                ep = u % N_EP
                if kc == 0 and 2 <= u <= N_QB + 1:
                    i = u - 2
                    te.wait_ge(s_v.h, V_QLD[i])
                    if i >= 2:
                        te.wait_ge(s_v.h, V_QTE[i - 2])
                    ps = P4 if i % 2 == 0 else P5
                    for cch in range(8):
                        ins = nc.tensor.transpose(ps[:, cch * 128:(cch + 1) * 128],
                                                  xdst[:, cch * 128:(cch + 1) * 128],
                                                  identbf[:])
                    ins.then_inc(s_t.h, 1)
                te.wait_ge(s_kv.h, A_k(g))
                if g >= 2:
                    te.wait_ge(s_ex.h, g - 1)
                Sg = P0 if g % 2 == 0 else P1
                kt = ktb[(g // 4) % 4]
                ins = te.matmul(Sg[:, 0:512],
                                kt[0:64, (kc % 4) * 128:(kc % 4 + 1) * 128],
                                T3[0:64, hp, ep * 512:(ep + 1) * 512],
                                start=True, stop=True, tile_position=(0, 0), **MM)
                ins = te.matmul(Sg[:, 512:1024],
                                kt[64:128, (kc % 4) * 128:(kc % 4 + 1) * 128],
                                T3[64:128, hp, ep * 512:(ep + 1) * 512],
                                start=True, stop=True, tile_position=(64, 0), **MM)
                ins.then_inc(s_qk.h, 1)
                if g >= 2:
                    gm = g - 2
                    um = gm // 64
                    te.wait_ge(s_ex.h, gm + 1)
                    te.wait_ge(s_kv.h, kv_after(gm))
                    if gm % 64 == 0:
                        te.wait_ge(s_ev.h, um)
                    ins = te.matmul(P2[0:65, :], vtb[gm % 8][:, 0:65],
                                    Pb[gm % 4][:, 0:512],
                                    start=(gm % 64 == 0), stop=(gm % 64 == 63),
                                    **MM)
                    ins = te.matmul(P3[0:65, :], vtb[gm % 8][:, 65:130],
                                    Pb[gm % 4][:, 512:1024],
                                    start=(gm % 64 == 0), stop=(gm % 64 == 63),
                                    **MM)
                    ins.then_inc(s_pv.h, 1)
            for gm in range(TOT_ITER - 2, TOT_ITER):
                um = gm // 64
                te.wait_ge(s_ex.h, gm + 1)
                te.wait_ge(s_kv.h, kv_after(gm))
                if gm % 64 == 0:
                    te.wait_ge(s_ev.h, um)
                ins = te.matmul(P2[0:65, :], vtb[gm % 8][:, 0:65],
                                Pb[gm % 4][:, 0:512],
                                start=(gm % 64 == 0), stop=(gm % 64 == 63), **MM)
                ins = te.matmul(P3[0:65, :], vtb[gm % 8][:, 65:130],
                                Pb[gm % 4][:, 512:1024],
                                start=(gm % 64 == 0), stop=(gm % 64 == 63), **MM)
                ins.then_inc(s_pv.h, 1)
        s_qk.v = TOT_ITER
        s_pv.v = TOT_ITER
        s_t.v = C_T_END

        @block.scalar
        def _(sc):
            sc.wait_ge(s_v.h, V_C3)
            for g in range(TOT_ITER):
                sc.wait_ge(s_qk.h, g + 1)
                if g >= 4:
                    sc.wait_ge(s_pv.h, g - 3)
                Sg = P0 if g % 2 == 0 else P1
                hp = g // (N_EP * N_KC)
                sc.activation(out=Pb[g % 4][:], in_=Sg[:],
                              func=AF.Exp, bias=negc8p[:, hp:hp + 1],
                              scale=0.125).then_inc(s_ex.h, 1)
        s_ex.v = TOT_ITER

        @block.gpsimd
        def _(g):
            for u in range(32):
                g.wait_ge(s_ev.h, u + 1)
                hp = u // N_EP
                ep = u % N_EP
                g.dma_start(out=Z_dram[2 * hp:2 * hp + 1, ep * 512:(ep + 1) * 512],
                            in_=Ztmp[0]).then_inc(s_g.h, 16)
                g.dma_start(out=Z_dram[2 * hp + 1:2 * hp + 2,
                                       ep * 512:(ep + 1) * 512],
                            in_=Ztmp[1]).then_inc(s_g.h, 16)
                if u < N_QB:
                    if u >= 2:
                        g.wait_ge(s_v.h, V_QLD[u - 2])
                    if u < 16:
                        g.dma_start(out=xin[u % 2],
                                    in_=xn_dram[u * 128:(u + 1) * 128, :]
                                    ).then_inc(s_g.h, 16)
                    else:
                        wt = u - 16
                        g.dma_start(out=QE, in_=ffnw_d[wt * 128:(wt + 1) * 128, :]
                                    ).then_inc(s_g.h, 16)
        s_g.v = C_G_END

        @block.vector
        def _(v):
            for u in range(32):
                hp = u // N_EP
                ep = u % N_EP
                v.wait_ge(s_pv.h, 64 * (u + 1))
                if u >= 1:
                    v.wait_ge(s_g.h, G_ZQ[u - 1])
                cs = slice(hp * 2048 + ep * 512, hp * 2048 + (ep + 1) * 512)
                v.tensor_copy(out=T2[0:64, :, :].rearrange("p a b -> p (a b)")
                              [:, cs], in_=P2[0:64, :])
                v.tensor_copy(out=T2[64:128, :, :].rearrange("p a b -> p (a b)")
                              [:, cs], in_=P3[0:64, :])
                v.tensor_copy(out=Ztmp[0], in_=P2[64:65, :])
                v.tensor_copy(out=Ztmp[1], in_=P3[64:65, :]).then_inc(s_ev.h, 1)
                if u < N_QB:
                    v.wait_ge(s_g.h, G_ZQ[u])
                    if u >= 1:
                        v.wait_ge(s_t.h, T_QTR[u - 1])
                    src = xin[u % 2] if u < 16 else QE
                    v.reduce_max(out=wsc[:],
                                 in_=src.rearrange("p (b c) -> p b c", c=32),
                                 axis=AX,
                                 apply_absolute_value=True).then_inc(s_v.h, 1)
                    v.wait_ge(s_v.h, V_QSC[u] - 2)
                    v.tensor_scalar(out=wsc[:], in0=wsc[:], scalar1=r75[:, 0:1],
                                    scalar2=1e-30, op0=ALU.mult,
                                    op1=ALU.max).then_inc(s_v.h, 1)
                    v.wait_ge(s_v.h, V_QSC[u] - 1)
                    v.reciprocal(out=wrc[:], in_=wsc[:]).then_inc(s_v.h, 1)
                    v.wait_ge(s_v.h, V_QSC[u])
                    ladder(v, src, wrc[:], wsc[:], QA, QB, QC, QD,
                           xdst).then_inc(s_v.h, 1)
                if u >= 2 and (u - 2) < N_QB:
                    i = u - 2
                    v.wait_ge(s_t.h, T_QTR[i])
                    ps = P4 if i % 2 == 0 else P5
                    dstT = T1 if i < 16 else WAr
                    col = i * 128 if i < 16 else (i - 16) * 128
                    v.tensor_copy(out=dstT[:, :, col:col + 128],
                                  in_=ps[:].rearrange("p (c t) -> p c t", c=8)
                                  ).then_inc(s_v.h, 1)
        s_v.v = C_V_END
        s_ev.v = 32

        if debug:
            @block.gpsimd
            def _(g):
                g.wait_ge(s_v.h, C_V_END)
                g.wait_ge(s_ev.h, 32)
                g.wait_ge(s_g.h, C_G_END)
                g.dma_start(out=dbg["xdT"], in_=T1[:]).then_inc(s_g.h, 16)
                g.dma_start(out=dbg["wdT"], in_=WAr[:]).then_inc(s_g.h, 16)
                g.dma_start(out=dbg["aoT"], in_=T2[:]).then_inc(s_g.h, 16)
                g.dma_start(out=dbg["negc8"], in_=negc8h).then_inc(s_g.h, 16)
                g.dma_start(out=dbg["beff"], in_=beff[:]).then_inc(s_g.h, 16)
            s_g.v += 16 * 5
        G_DBG_C = s_g.v

        if stop_after == "C":
            emit_tail([(s_v, s_v.v), (s_t, s_t.v), (s_a, s_a.v),
                       (s_g, s_g.v), (s_x, s_x.v), (s_ev, s_ev.v),
                       (s_pv, s_pv.v), (s_ex, s_ex.v), (s_kv, s_kv.v)])
            return nc

        @block.gpsimd
        def _(g, pvv=s_pv.v, cge=G_DBG_C, cve=C_V_END):
            g.wait_ge(s_pv.h, pvv)
            for icc in range(8):
                g.dma_start(out=WoT_sb[:, icc, :],
                            in_=WoT_d[icc * 128:(icc + 1) * 128, :]
                            ).then_inc(s_g.h, 16)
            g.dma_start(out=bo_bf, in_=bo_d).then_inc(s_g.h, 16)
            g.wait_ge(s_g.h, cge)
            g.dma_start(out=rZ, in_=Z_dram).then_inc(s_g.h, 16)
            g.wait_ge(s_v.h, cve)
            g.dma_start(out=indh[:], in_=indh_d).then_inc(s_g.h, 16)
        s_g.v += 16 * 11
        G_WOT = s_g.v

        @block.vector
        def _(v, gw=G_WOT):
            v.wait_ge(s_g.h, gw)
            v.tensor_scalar_add(out=rZ[:], in0=rZ[:], scalar1=1e-38)
            v.reciprocal(out=rZ[:], in_=rZ[:]).then_inc(s_v.h, 1)
        s_v.v += 1
        V_RZ = s_v.v

        d2_t0 = s_t.v
        d2_v0 = s_v.v
        @block.tensor
        def _(te, t0=d2_t0, v0=d2_v0):
            te.wait_ge(s_v.h, V_RZ)
            for p in range(16):
                icc, tp = p // 2, p % 2
                if p >= 2:
                    te.wait_ge(s_v.h, v0 + (p - 2) + 1)
                bp = P0 if p % 2 == 0 else P1
                te.matmul(bp[:, 0:512], indh[:, icc, :],
                          rZ[0:16, tp * 1024:tp * 1024 + 512],
                          start=True, stop=True, **MM)
                te.matmul(bp[:, 512:1024], indh[:, icc, :],
                          rZ[0:16, tp * 1024 + 512:(tp + 1) * 1024],
                          start=True, stop=True, **MM).then_inc(s_t.h, 1)
        s_t.v += 16

        @block.vector
        def _(v, t0=d2_t0):
            for p in range(16):
                icc, tp = p // 2, p % 2
                v.wait_ge(s_t.h, t0 + p + 1)
                bp = P0 if p % 2 == 0 else P1
                cs = slice(icc * 2048 + tp * 1024, icc * 2048 + (tp + 1) * 1024)
                v.tensor_mul(out=T2[:].rearrange("p a b -> p (a b)")[:, cs],
                             in0=T2[:].rearrange("p a b -> p (a b)")[:, cs],
                             in1=bp[:]).then_inc(s_v.h, 1)
        s_v.v += 16
        V_AOS = s_v.v

        @block.tensor
        def _(te, gv=G_WOT, vv=V_AOS):
            te.wait_ge(s_g.h, gv)
            te.wait_ge(s_v.h, vv)
            for icc in range(8):
                ins = te.matmul(P2[0:1, :], bv_bf[:, icc:icc + 1],
                                WoT_sb[:, icc, 0:512],
                                start=(icc == 0), stop=(icc == 7), **MM)
                ins = te.matmul(P3[0:1, :], bv_bf[:, icc:icc + 1],
                                WoT_sb[:, icc, 512:1024],
                                start=(icc == 0), stop=(icc == 7), **MM)
            ins.then_inc(s_t.h, 1)
        s_t.v += 1

        @block.vector
        def _(v, tv=s_t.v):
            v.wait_ge(s_t.h, tv)
            v.memset(c_sc[:, 0:1], 1.0)
            v.tensor_add(out=beff[:, 0:512], in0=bo_bf[:, 0:512], in1=P2[0:1, :])
            v.tensor_add(out=beff[:, 512:1024], in0=bo_bf[:, 512:1024],
                         in1=P3[0:1, :]).then_inc(s_v.h, 1)
        s_v.v += 1

        @block.tensor
        def _(te, vv=s_v.v):
            te.wait_ge(s_v.h, vv)
            for j in range(8):
                ins = nc.tensor.transpose(P2[:, j:j + 1],
                                          beff[0:1, j * 128:(j + 1) * 128],
                                          c_sc[0:1, 0:1])
            ins.then_inc(s_t.h, 1)
        s_t.v += 1

        @block.vector
        def _(v, tv=s_t.v):
            v.wait_ge(s_t.h, tv)
            v.tensor_copy(out=boeff_pc[:], in_=P2[:, 0:8]).then_inc(s_v.h, 1)
        s_v.v += 1
        V_BO = s_v.v

        tsc = [QF[:, 6144:6656], QF[:, 6656:7168]]
        fsc = [QF[:, 7168:7680], QF[:, 7680:8192]]
        gsc4 = QF[:, 4096:6144]
        d_t0 = s_t.v
        d_a0 = s_a.v
        d_v0 = s_v.v
        TQ = lambda grp, t4: d_t0 + 4 * grp + t4 + 1
        AQ = lambda grp, t4: d_a0 + 4 * grp + t4 + 1
        VQ = lambda grp, t4: d_v0 + 4 * grp + t4 + 1

        VQO = lambda oc, t4: d_v0 + 4 * oc + t4 + 1

        @block.tensor
        def _(te, vv=s_v.v):
            te.wait_ge(s_v.h, vv)
            for grp in range(16):
                oc = grp // 2
                ffn = (grp % 2 == 0)
                for t4 in range(4):
                    bp = BK4[t4]
                    if grp >= 1:
                        te.wait_ge(s_a.h, AQ(grp - 1, t4))
                    for icc in range(8):
                        if ffn:
                            ins = te.matmul(bp,
                                            WAr[:, icc, oc * 128:(oc + 1) * 128],
                                            T1[:, icc, t4 * 512:(t4 + 1) * 512],
                                            start=(icc == 0), stop=(icc == 7),
                                            **MM)
                        else:
                            tf = T2[:].rearrange("p a b -> p (a b)")
                            ins = te.matmul(bp,
                                            WoT_sb[:, icc,
                                                   oc * 128:(oc + 1) * 128],
                                            tf[:, icc * 2048 + t4 * 512:
                                               icc * 2048 + (t4 + 1) * 512],
                                            start=(icc == 0), stop=(icc == 7),
                                            **MM)
                    ins.then_inc(s_t.h, 1)
        s_t.v += 64

        @block.scalar
        def _(sc):
            for grp in range(16):
                oc = grp // 2
                ffn = (grp % 2 == 0)
                for t4 in range(4):
                    sc.wait_ge(s_t.h, TQ(grp, t4))
                    if ffn:
                        if oc >= 1:
                            sc.wait_ge(s_v.h, VQO(oc - 1, t4))
                        sc.activation(out=gsc4[:, t4 * 512:(t4 + 1) * 512],
                                      in_=BK4[t4], func=AF.Silu,
                                      bias=ffnb_s[:, oc:oc + 1]
                                      ).then_inc(s_a.h, 1)
                    else:
                        fl = 4 * oc + t4
                        if fl >= 2:
                            sc.wait_ge(s_v.h, d_v0 + fl - 1)
                        sc.activation(out=tsc[t4 % 2], in_=BK4[t4],
                                      func=AF.Identity,
                                      bias=boeff_pc[:, oc:oc + 1]
                                      ).then_inc(s_a.h, 1)
        s_a.v += 64

        @block.vector
        def _(v):
            for oc in range(8):
                for t4 in range(4):
                    v.wait_ge(s_a.h, AQ(2 * oc + 1, t4))
                    cs = slice(t4 * 512, (t4 + 1) * 512)
                    v.tensor_add(out=tsc[t4 % 2], in0=tsc[t4 % 2],
                                 in1=T4[:, oc, cs])
                    v.tensor_mul(out=T3[:, oc, cs],
                                 in0=tsc[t4 % 2],
                                 in1=gsc4[:, cs]).then_inc(s_v.h, 1)
        s_v.v += 32
        V_PT = s_v.v

        if debug:
            @block.gpsimd
            def _(g):
                g.wait_ge(s_v.h, V_PT)
                g.dma_start(out=dbg["pT"], in_=T3[:]).then_inc(s_g.h, 16)
            s_g.v += 16

        d7_t0 = s_t.v
        d8_x0 = s_x.v
        d8_v0 = s_v.v
        d8_g0 = s_g.v

        @block.tensor
        def _(te, t0=d7_t0, v0=d8_v0, vv=V_PT):
            te.wait_ge(s_v.h, vv)
            for j in range(16):
                if j >= 2:
                    te.wait_ge(s_v.h, v0 + (j - 2) + 1)
                ps = P4 if j % 2 == 0 else P5
                for oc in range(8):
                    ins = nc.tensor.transpose(ps[:, oc * 128:(oc + 1) * 128],
                                              T3[:, oc, j * 128:(j + 1) * 128],
                                              identbf[:])
                ins.then_inc(s_t.h, 1)
        s_t.v += 16

        @block.sync
        def _(sy, v0=d8_v0, t0=d2_t0):
            for j in range(16):
                if j == 0:
                    sy.wait_ge(s_t.h, t0 + 16)
                if j >= 2:
                    sy.wait_ge(s_v.h, v0 + (j - 2) + 1)
                sy.dma_start(out=xin[j % 2],
                             in_=x_own[j * 128:(j + 1) * 128, :]
                             ).then_inc(s_x.h, 16)
        s_x.v += 16 * 16

        @block.vector
        def _(v, x0=d8_x0, g0=d8_g0, t0=d7_t0):
            for j in range(16):
                v.wait_ge(s_x.h, x0 + 16 * (j + 1))
                v.wait_ge(s_t.h, t0 + j + 1)
                if j >= 2:
                    v.wait_ge(s_g.h, g0 + 16 * (j - 1))
                ps = P4 if j % 2 == 0 else P5
                v.tensor_add(out=xo_t[j % 2].rearrange("p (c t) -> p c t", c=8),
                             in0=xin[j % 2].rearrange("p (c t) -> p c t", c=8),
                             in1=ps[:].rearrange("p (c t) -> p c t", c=8)
                             ).then_inc(s_v.h, 1)
        s_v.v += 16

        @block.gpsimd
        def _(g, v0=d8_v0, g0=d8_g0):
            for j in range(16):
                g.wait_ge(s_v.h, v0 + j + 1)
                g.dma_start(out=out_d[j * 128:(j + 1) * 128, :],
                            in_=xo_t[j % 2]).then_inc(s_g.h, 16)
            g.wait_ge(s_g.h, g0 + 16 * 16)
        s_g.v += 16 * 16

    return nc



def _indf_const():
    a = np.zeros((128, 8, 16), np.float32)
    for p in range(128):
        for ic in range(8):
            a[p, ic, 2 * ic + (1 if p >= 64 else 0)] = 1.0
    return a.astype(ml_dtypes.bfloat16)


def _indh_const():
    a = np.zeros((16, 8, 128), np.float32)
    for h in range(16):
        for c in range(8):
            for m in range(128):
                if 2 * c + (1 if m >= 64 else 0) == h:
                    a[h, c, m] = 1.0
    return a


def _prep_host(inputs):
    x = np.asarray(inputs["L0_attn_x"], np.float32)
    lnw = np.asarray(inputs["L0_attn_ln_w"], np.float32)
    lnb = np.asarray(inputs["L0_attn_ln_b"], np.float32)
    Win = np.asarray(inputs["L0_attn_in_proj_w"], np.float32)
    bin_ = np.asarray(inputs["L0_attn_in_proj_b"], np.float32)
    Wo = np.asarray(inputs["L0_attn_out_proj_w"], np.float32)
    bo = np.asarray(inputs["L0_attn_out_proj_b"], np.float32)
    w0 = np.asarray(inputs["L0_attn_w0"], np.float32)
    w1 = np.asarray(inputs["L0_attn_w1"], np.float32)
    w2 = np.asarray(inputs["L0_attn_w2"], np.float32)
    fw = np.asarray(inputs["L0_ffn_weight"], np.float32)
    fb = np.asarray(inputs["L0_ffn_bias"], np.float32)

    def pc(vec):
        return np.ascontiguousarray(vec.reshape(8, 128).T)

    def packw(wh):
        return np.ascontiguousarray(
            wh.reshape(4, 2, 128, 256).transpose(2, 0, 1, 3).reshape(128, 8, 256))

    w0T = np.ascontiguousarray(np.swapaxes(w0, 1, 2))
    w2T = np.ascontiguousarray(np.swapaxes(w2, 1, 2))
    w1T = np.ascontiguousarray(np.swapaxes(w1, 1, 2))

    common = dict(
        lnw_pc=pc(lnw), lnb_pc=pc(lnb),
        lnw_row=np.ascontiguousarray(lnw.reshape(1, -1)),
        lnb_row=np.ascontiguousarray(lnb.reshape(1, -1)),
        WinT=np.ascontiguousarray(Win.T),
        bin_row=np.ascontiguousarray(bin_.reshape(1, -1)),
        WoT=np.ascontiguousarray(Wo.T),
        bo_row=np.ascontiguousarray(bo.reshape(1, -1)),
        w0T_p=packw(w0T), w1_p=packw(w1), w1T_p=packw(w1T), w2T_p=packw(w2T),
        ffn_w=np.ascontiguousarray(fw),
        ffn_b_pc=pc(fb),
        ident_bf=np.eye(128, dtype=np.float32).astype(ml_dtypes.bfloat16),
        indf_in=_indf_const(), indh_in=_indh_const(),
    )
    maps = []
    for core in range(8):
        b = core // 4
        s = core % 4
        xb = x[b]
        perm = np.roll(np.arange(4), 3 - s)
        m = dict(common)
        m["x_full"] = np.ascontiguousarray(
            xb.reshape(4, TOWN, D)[perm].reshape(S, D))
        m["x_own"] = np.ascontiguousarray(xb[s * TOWN:(s + 1) * TOWN])
        maps.append(m)
    return maps


_NC_CACHE = {}


def kernel(**inputs):
    if "prog" not in _NC_CACHE:
        _NC_CACHE["prog"] = build(debug=False)
    nc = _NC_CACHE["prog"]
    maps = _prep_host(inputs)
    res = run_bass_kernel_spmd(nc, maps, core_ids=list(range(8)))
    out = np.empty((B_, S, D), np.float32)
    for core in range(8):
        b, s = core // 4, core % 4
        out[b, s * TOWN:(s + 1) * TOWN] = res.results[core]["out"]
    return out
